# version 22
# baseline (speedup 1.0000x reference)
import os
import numpy as np
from contextlib import ExitStack

import concourse.bass as bass
import concourse.mybir as mybir
import bass_rust as _br
from concourse import library_config
from concourse.bass_utils import run_bass_kernel_spmd

F32 = mybir.dt.float32
BF16 = mybir.dt.bfloat16
AF = mybir.ActivationFunctionType
ALU = mybir.AluOpType

NCORE = 8
B, L, D, H, DH = 16, 2048, 1024, 16, 64
DF = 4096
DM = 8192
NCH = D // 128
T1 = 512
NT1 = L // T1
LNEPS = 1e-5

DEBUG = int(os.environ.get("KMK_DEBUG", "0"))
PHASES = os.environ.get("KMK_PHASES", "all")
NPASS = int(os.environ.get("KMK_NPASS", "2"))



def _pp(vec, nch):
    return np.ascontiguousarray(vec.reshape(nch, 128).T)


def _pair_pos(stage):
    s = 2 ** stage
    k = np.arange(DF // 2)
    pa = (k // s) * 2 * s + (k % s)
    pb = pa + s
    return pa, pb


def _stage_mats(weights, stage):
    pa, pb = _pair_pos(stage)
    A = np.zeros((DF // 128, 128, 128), dtype=np.float32)
    blk = pa // 128
    pal, pbl = pa % 128, pb % 128
    w = weights
    A[blk, pal, pal] = w[:, 0, 0]
    A[blk, pbl, pal] = w[:, 0, 1]
    A[blk, pal, pbl] = w[:, 1, 0]
    A[blk, pbl, pbl] = w[:, 1, 1]
    AT = np.ascontiguousarray(np.transpose(A, (0, 2, 1)))
    return A, AT


def _cross_coeffs(weights, stage, kind):
    pa, pb = _pair_pos(stage)
    self_c = np.zeros(DF, dtype=np.float32)
    part_c = np.zeros(DF, dtype=np.float32)
    w = weights
    self_c[pa] = w[:, 0, 0]
    self_c[pb] = w[:, 1, 1]
    if kind == "right":
        part_c[pa] = w[:, 0, 1]
        part_c[pb] = w[:, 1, 0]
    else:
        part_c[pa] = w[:, 1, 0]
        part_c[pb] = w[:, 0, 1]
    return self_c, part_c


def _prep_core_inputs(inputs, core):
    ip = {k: np.asarray(v) for k, v in inputs.items()}
    g = lambda k: ip[k]
    m = {}
    b0, b1 = core, core + 8
    x = g("L0_attn_x")
    m["x_fm"] = np.ascontiguousarray(
        np.stack([x[b0].T, x[b1].T]))

    for nm in ["Wq", "Wk", "Wv", "Wo"]:
        m["l0_" + nm] = np.ascontiguousarray(g(f"L0_attn_{nm}"))
    m["l0_bq"] = _pp(g("L0_attn_bq"), NCH)
    m["l0_bo"] = _pp(g("L0_attn_bo"), NCH)
    m["l0_bk_row"] = np.ascontiguousarray(g("L0_attn_bk")[None, :])
    m["l0_bv_row"] = np.ascontiguousarray(g("L0_attn_bv")[None, :])

    m["l0_WexpT"] = np.ascontiguousarray(g("L0_ffn_W_exp").T)
    m["l0_bexp_pp"] = _pp(g("L0_ffn_b_exp"), 32)
    m["l0_Wcon"] = np.ascontiguousarray(g("L0_ffn_W_con"))
    m["l0_up_bias"] = _pp(g("L0_ffn_up_bias"), DF // 128)
    m["l0_bcon"] = _pp(g("L0_ffn_b_con"), NCH)
    m["l0_dbias"] = _pp(g("L0_ffn_down_bias"), NCH)

    up_w, dn_w = g("L0_ffn_up_weights"), g("L0_ffn_down_weights")
    up_mats = []
    for s in range(7):
        A, AT = _stage_mats(up_w[s], s)
        up_mats.append(A if s == 6 else AT)
    seq = [up_mats[6], up_mats[5], up_mats[4], up_mats[3],
           up_mats[2], up_mats[1], up_mats[0]]
    m["bfu_mats"] = np.ascontiguousarray(
        np.stack(seq, axis=1))
    dn_mats = []
    for s in range(7):
        A, AT = _stage_mats(dn_w[s], s)
        dn_mats.append(AT if s == 0 else A)
    seq = [dn_mats[0], dn_mats[1], dn_mats[2], dn_mats[3],
           dn_mats[4], dn_mats[5], dn_mats[6]]
    m["bfd_mats"] = np.ascontiguousarray(np.stack(seq, axis=1))

    ucs, ucp, dcs, dcp = [], [], [], []
    for s in range(7, 12):
        sc, pc = _cross_coeffs(up_w[s], s, "right")
        ucs.append(_pp(sc, 32))
        ucp.append(_pp(pc, 32))
        sc, pc = _cross_coeffs(dn_w[s], s, "left")
        dcs.append(_pp(sc, 32))
        dcp.append(_pp(pc, 32))
    m["bfu_self"] = np.ascontiguousarray(np.concatenate(ucs, axis=1))
    m["bfu_part"] = np.ascontiguousarray(np.concatenate(ucp, axis=1))
    m["bfd_self"] = np.ascontiguousarray(np.concatenate(dcs, axis=1))
    m["bfd_part"] = np.ascontiguousarray(np.concatenate(dcp, axis=1))

    for nm in ["Wq", "Wk", "Wv"]:
        w = g(f"L1_attn_{nm}")
        m["l1_" + nm] = np.ascontiguousarray(
            np.transpose(w, (1, 0, 2)).reshape(D, H * DH))
    m["l1_projT"] = np.ascontiguousarray(g("L1_attn_proj_w").T)
    m["l1_projb"] = _pp(g("L1_attn_proj_b"), NCH)

    m["l1_w1"] = np.ascontiguousarray(g("L1_ffn_w1"))
    m["l1_w2"] = np.ascontiguousarray(g("L1_ffn_w2"))
    ew = g("L1_ffn_expert_weights")
    sl0 = ew[core * 256:(core + 1) * 256]
    sl1 = ew[2048 + core * 256:2048 + (core + 1) * 256]
    m["l1_ws2"] = np.ascontiguousarray(
        np.concatenate([sl0, sl1], axis=0).T)

    cv = np.zeros((128, 8), dtype=np.float32)
    cv[:, 0] = 1e-5
    cv[:, 1] = 1.0 / D
    cv[:, 2] = -1.0 / D
    cv[:, 3] = 1.0 / float(L * DH)
    cv[:, 4] = 0.125
    cv[:, 5] = -1e30
    cv[:, 6] = -1.0
    m["cvals"] = cv
    m["identity"] = np.eye(128, dtype=np.float32)
    m["ones_col"] = np.ones((128, 1), dtype=np.float32)
    m["ones2"] = np.ones((2, 128), dtype=np.float32)
    p = np.arange(128)[:, None]
    mcol = np.arange(2080)[None, :]
    m["l1_maskbig"] = np.ascontiguousarray(
        np.where(p * 16 > mcol - 15, np.float32(-1e30), np.float32(0.0))
        .astype(np.float32))
    return m



class Sem:
    def __init__(self, es, nc, name, step=1):
        self.ap = es.enter_context(nc.semaphore(name))
        self.step = step
        self.n = 0

    def inc(self, instr, k=1):
        instr.then_inc(self.ap, self.step * k)
        self.n += self.step * k
        return self.n


def build_program():
    nc = bass.Bass("TRN2", target_bir_lowering=False)

    x_fm = nc.dram_tensor("x_fm", [2, D, L], F32, kind="ExternalInput").ap()
    out = nc.dram_tensor("out", [2, D, L], F32, kind="ExternalOutput").ap()
    dram = {}
    for nm, shp in [
        ("l0_Wq", [D, D]), ("l0_Wk", [D, D]), ("l0_Wv", [D, D]), ("l0_Wo", [D, D]),
        ("l0_bq", [128, NCH]), ("l0_bo", [128, NCH]),
        ("l0_bk_row", [1, D]), ("l0_bv_row", [1, D]),
        ("l0_WexpT", [DF, D]), ("l0_bexp_pp", [128, 32]), ("l0_Wcon", [DF, D]),
        ("l0_up_bias", [128, 32]), ("l0_bcon", [128, NCH]), ("l0_dbias", [128, NCH]),
        ("bfu_mats", [32, 7, 128, 128]), ("bfd_mats", [32, 7, 128, 128]),
        ("bfu_self", [128, 5 * 32]), ("bfu_part", [128, 5 * 32]),
        ("bfd_self", [128, 5 * 32]), ("bfd_part", [128, 5 * 32]),
        ("l1_Wq", [D, D]), ("l1_Wk", [D, D]), ("l1_Wv", [D, D]),
        ("l1_projT", [D, D]), ("l1_projb", [128, NCH]),
        ("l1_w1", [D, DM]), ("l1_w2", [DM, D]),
        ("l1_ws2", [2, 512]),
        ("identity", [128, 128]), ("ones_col", [128, 1]),
        ("ones2", [2, 128]),
        ("l1_maskbig", [128, 2080]), ("cvals", [128, 8]),
    ]:
        dram[nm] = nc.dram_tensor(nm, shp, F32, kind="ExternalInput").ap()
    wexp_f = nc.dram_tensor("wexp_fold", [D, DF], F32, kind="Internal").ap()
    wexp_h = nc.dram_tensor("wexp_hi", [D, DF], BF16, kind="Internal").ap()
    wexp_l = nc.dram_tensor("wexp_lo", [D, DF], BF16, kind="Internal").ap()
    a2a_in = nc.dram_tensor("a2a_in", [8, D, 256], F32, kind="Internal")
    a2a_out = nc.dram_tensor("a2a_out", [8, D, 256], F32, kind="Internal")
    moe_out = nc.dram_tensor("moe_out", [D, 512], F32, kind="ExternalOutput").ap()
    wcon_f = nc.dram_tensor("wcon_fold", [DF, D], F32, kind="Internal").ap()
    dbg = {}
    if DEBUG:
        for nm, shp in [("dbg_x1", [D, L]), ("dbg_x2", [D, L]), ("dbg_x3", [D, L]),
                        ("dbg_kv", [64, H * DH]), ("dbg_xln", [D, T1]),
                        ("dbg_wexpf", [D, DF]), ("dbg_wconf", [DF, D]),
                        ("dbg_bexp", [128, 32])]:
            dbg[nm] = nc.dram_tensor(nm, shp, F32, kind="ExternalOutput").ap()

    es = ExitStack()
    with es:
        sb = lambda name, shp: es.enter_context(nc.sbuf_tensor(name, shp, F32)).ap()
        xres = sb("xres", [128, NCH * L])
        xln = sb("xln", [128, NCH * T1])
        ident = sb("ident", [128, 128])
        bq = sb("bq", [128, NCH]); bo = sb("bo", [128, NCH])
        projb = sb("projb", [128, NCH])
        bcd = sb("bcd", [128, 2 * NCH])
        upb = sb("upb", [128, 32])
        btot = sb("btot", [128, 32])
        ones = sb("ones", [128, 1])
        ones2 = sb("ones2t", [2, 128])
        onesr = ones2[0:1, :]
        cval = sb("cval", [128, 8])
        for k, v in [(0, 1e-5), (1, 1.0 / D), (2, -1.0 / D),
                     (3, 1.0 / float(L * DH)), (4, 0.125), (5, -1e30),
                     (6, -1.0)]:
            nc.const_aps.register(F32, v, cval[:, k:k + 1])
        st_m = sb("st_m", [1, T1])
        st_q = sb("st_q", [1, T1])
        st_nm = sb("st_nm", [1, T1])
        nmb = sb("nmb", [128, T1]); rsb = sb("rsb", [128, T1])
        G1 = sb("G1", [128, 8192])
        G2 = sb("G2", [128, 8192])
        G3 = sb("G3", [128, 4096])
        G4 = sb("G4", [128, 5120])
        G5 = sb("G5", [128, 2048])
        G6 = sb("G6", [128, 2080])
        kvacc = G4[0:64, 4096:5120]
        coef = G4[:, 1280:1920]

        pb = [es.enter_context(nc.psum_tensor(f"pb{i}", [128, 512], F32)).ap()
              for i in range(8)]

        sems = {}
        for nm, step in [("ld", 16), ("st", 16), ("pe", 1), ("dve", 1),
                         ("act", 1), ("gp", 1), ("oo", 16),
                         ("wb0", 16), ("wb1", 16),
                         ("wc0", 16), ("wc1", 16), ("wc2", 16), ("wc3", 16)]:
            sems[nm] = Sem(es, nc, "s_" + nm, step)
        sLD, sST, sPE, sDVE, sACT, sGP, sOO = (sems[k] for k in
                                               ["ld", "st", "pe", "dve",
                                                "act", "gp", "oo"])
        sWB = [sems["wb0"], sems["wb1"]]
        sWC = [sems["wc0"], sems["wc1"], sems["wc2"], sems["wc3"]]

        blk = es.enter_context(nc.Block())

        def section(engname):
            dec = getattr(blk, engname)
            def wrap(fn):
                dec(fn)
            return wrap

        def barrier():
            snap = [(s, s.n) for s in sems.values() if s.n > 0]
            for engname in ["sync", "tensor", "vector", "scalar", "gpsimd"]:
                @section(engname)
                def _(q, snap=snap):
                    for s, v in snap:
                        q.wait_ge(s.ap, v)

        @section("sync")
        def _(q):
            for nm, dst in [("identity", ident), ("cvals", cval),
                            ("l0_bq", bq), ("l0_bo", bo),
                            ("l1_projb", projb), ("l0_up_bias", upb),
                            ("ones_col", ones), ("ones2", ones2)]:
                sLD.inc(q.dma_start(out=dst, in_=dram[nm]))
            sLD.inc(q.dma_start(out=bcd[:, 0:NCH], in_=dram["l0_bcon"]))
            sLD.inc(q.dma_start(out=bcd[:, NCH:2 * NCH], in_=dram["l0_dbias"]))
        ld_const = sLD.n

        @section("vector")
        def _(v):
            v.wait_ge(sLD.ap, ld_const)
            sDVE.inc(v.memset(kvacc, 0.0))
            sDVE.inc(v.tensor_tensor(bcd[:, 0:NCH], bcd[:, 0:NCH],
                                     bcd[:, NCH:2 * NCH], ALU.add))

        barrier()

        def emit_ln_stats(t0, T, src=None, sstr=None, ld_wait=None,
                          dve_free=None):
            if src is None:
                src, sstr = xres, L
            dve_prior = sDVE.n if dve_free is None else dve_free

            @section("tensor")
            def _(t):
                if ld_wait is not None:
                    t.wait_ge(sLD.ap, ld_wait)
                t.wait_ge(sDVE.ap, dve_prior)
                for c in range(NCH):
                    i = t.matmul(pb[0][0:1, 0:T], ones,
                                 src[:, c * sstr + t0:c * sstr + t0 + T],
                                 start=(c == 0), stop=(c == NCH - 1))
                sPE.inc(i)
            pe_xs = sPE.n

            pe_sq = {}
            for c in range(NCH):
                scr = nmb if c % 2 == 0 else rsb

                @section("scalar")
                def _(a, c=c, scr=scr):
                    if c == 0 and ld_wait is not None:
                        a.wait_ge(sLD.ap, ld_wait)
                    if c < 2:
                        a.wait_ge(sDVE.ap, dve_prior)
                    else:
                        a.wait_ge(sPE.ap, pe_sq[c - 2])
                    sACT.inc(a.activation(
                        scr[:, 0:T], src[:, c * sstr + t0:c * sstr + t0 + T],
                        AF.Square))
                act_c = sACT.n

                @section("tensor")
                def _(t, c=c, scr=scr, act_c=act_c):
                    t.wait_ge(sACT.ap, act_c)
                    sPE.inc(t.matmul(pb[1][0:1, 0:T], ones, scr[:, 0:T],
                                     start=(c == 0), stop=(c == NCH - 1)))
                pe_sq[c] = sPE.n
            pe_sum = sPE.n

            @section("vector")
            def _(v):
                v.wait_ge(sPE.ap, pe_sum)
                v.tensor_scalar(st_m[:, 0:T], pb[0][0:1, 0:T], 1.0 / D, None,
                                ALU.mult)
                v.tensor_scalar(st_q[:, 0:T], pb[1][0:1, 0:T], 1.0 / D, None,
                                ALU.mult)
                sDVE.inc(v.tensor_scalar(st_nm[:, 0:T], pb[0][0:1, 0:T],
                                         -1.0 / D, None, ALU.mult))
                v.wait_ge(sDVE.ap, sDVE.n)
                sDVE.inc(v.tensor_tensor(st_m[:, 0:T], st_m[:, 0:T], st_m[:, 0:T],
                                         ALU.mult))
                v.wait_ge(sDVE.ap, sDVE.n)
                sDVE.inc(v.tensor_tensor(st_q[:, 0:T], st_q[:, 0:T],
                                         st_m[:, 0:T], ALU.subtract))
            dve_var = sDVE.n

            @section("scalar")
            def _(a):
                a.wait_ge(sDVE.ap, dve_var)
                sACT.inc(a.activation(st_q[:, 0:T], st_q[:, 0:T], AF.Sqrt,
                                      bias=LNEPS))
            act_sd = sACT.n

            @section("vector")
            def _(v):
                v.wait_ge(sACT.ap, act_sd)
                sDVE.inc(v.reciprocal(st_m[:, 0:T], st_q[:, 0:T]))
            dve_rs = sDVE.n

            @section("tensor")
            def _(t):
                t.wait_ge(sDVE.ap, dve_rs)
                sPE.inc(t.matmul(pb[0][:, 0:T], onesr, st_nm[:, 0:T],
                                 start=True, stop=True))
                sPE.inc(t.matmul(pb[1][:, 0:T], onesr, st_m[:, 0:T],
                                 start=True, stop=True))
            pe_bc = sPE.n

            @section("vector")
            def _(v):
                v.wait_ge(sPE.ap, pe_bc)
                v.tensor_copy(nmb[:, 0:T], pb[0][:, 0:T])
                sDVE.inc(v.tensor_copy(rsb[:, 0:T], pb[1][:, 0:T]))
            return sDVE.n

        def emit_ln_apply(t0, T, dve_stats, src=None, sstr=None):
            if src is None:
                src, sstr = xres, L
            pe_prior = sPE.n

            @section("vector")
            def _(v):
                v.wait_ge(sDVE.ap, dve_stats)
                if pe_prior:
                    v.wait_ge(sPE.ap, pe_prior)
                for c in range(NCH):
                    i = v.tensor_tensor(xln[:, c * T1:c * T1 + T],
                                        src[:, c * sstr + t0:c * sstr + t0 + T],
                                        nmb[:, 0:T], ALU.add)
                sDVE.inc(i)
                v.wait_ge(sDVE.ap, sDVE.n)
                for c in range(NCH):
                    i = v.tensor_tensor(xln[:, c * T1:c * T1 + T],
                                        xln[:, c * T1:c * T1 + T],
                                        rsb[:, 0:T], ALU.mult)
                sDVE.inc(i)

        def emit_ln(t0, T, tap=None, src=None, sstr=None, ld_wait=None):
            ds = emit_ln_stats(t0, T, src=src, sstr=sstr, ld_wait=ld_wait)
            emit_ln_apply(t0, T, ds, src=src, sstr=sstr)

        def phase_load_x(p):
            @section("sync")
            def _(q):
                for c in range(NCH):
                    i = q.dma_start(out=xres[:, c * L:(c + 1) * L],
                                    in_=x_fm[p, c * 128:(c + 1) * 128, :])
                    sLD.inc(i)
            barrier()

        def phase_kv(p):
            Wk, Wv = G1, G2
            ktok = G3
            vtok = G4[:, 0:4096]
            bkb = G5[:, 0:1024]
            bvb = G6[:, 0:1024]

            @section("sync")
            def _(q):
                sLD.inc(q.dma_start(out=Wk.rearrange("p (c n) -> p c n", c=NCH),
                                    in_=dram["l0_Wk"].rearrange("(c p) n -> p c n", p=128)))
                sLD.inc(q.dma_start(out=Wv.rearrange("p (c n) -> p c n", c=NCH),
                                    in_=dram["l0_Wv"].rearrange("(c p) n -> p c n", p=128)))
                sLD.inc(q.dma_start(out=G5[0:1, 1024:2048], in_=dram["l0_bk_row"]))
                sLD.inc(q.dma_start(out=G6[0:1, 1024:2048], in_=dram["l0_bv_row"]))
            ld_w = sLD.n

            for half in range(2):
                @section("tensor")
                def _(t, half=half):
                    t.wait_ge(sLD.ap, ld_w)
                    if half == 1:
                        t.wait_ge(sDVE.ap, sDVE.n)
                    sPE.inc(t.matmul(
                        pb[2][:, 0:512], onesr,
                        G5[0:1, 1024 + half * 512:1024 + half * 512 + 512],
                        start=True, stop=True))
                    sPE.inc(t.matmul(
                        pb[3][:, 0:512], onesr,
                        G6[0:1, 1024 + half * 512:1024 + half * 512 + 512],
                        start=True, stop=True))
                pe_bias = sPE.n

                @section("vector")
                def _(v, half=half, pe_bias=pe_bias):
                    v.wait_ge(sPE.ap, pe_bias)
                    v.tensor_copy(bkb[:, half * 512:half * 512 + 512],
                                  pb[2][:, 0:512])
                    sDVE.inc(v.tensor_copy(bvb[:, half * 512:half * 512 + 512],
                                           pb[3][:, 0:512]))

            @section("vector")
            def _(v):
                sDVE.inc(v.memset(kvacc, 0.0))

            ds_ln = emit_ln_stats(0, T1)
            dve_kvev = {0: 0, 1: 0}
            for ti in range(NT1):
                t0 = ti * T1
                emit_ln_apply(t0, T1, ds_ln)
                dve_ln = sDVE.n
                for tc in range(T1 // 128):
                    for half in range(2):
                        bankk = pb[2 + half]
                        bankv = pb[4 + half]

                        @section("tensor")
                        def _(t, tc=tc, half=half, bankk=bankk, bankv=bankv,
                              dve_ln=dve_ln, dve_bank=dve_kvev[half]):
                            t.wait_ge(sLD.ap, ld_w)
                            t.wait_ge(sDVE.ap, dve_ln)
                            if dve_bank:
                                t.wait_ge(sDVE.ap, dve_bank)
                            for c in range(NCH):
                                i = t.matmul(
                                    bankk[:, 0:512],
                                    xln[:, c * T1 + tc * 128:c * T1 + tc * 128 + 128],
                                    Wk[:, c * D + half * 512:c * D + half * 512 + 512],
                                    start=(c == 0), stop=(c == NCH - 1))
                            sPE.inc(i)
                            for c in range(NCH):
                                i = t.matmul(
                                    bankv[:, 0:512],
                                    xln[:, c * T1 + tc * 128:c * T1 + tc * 128 + 128],
                                    Wv[:, c * D + half * 512:c * D + half * 512 + 512],
                                    start=(c == 0), stop=(c == NCH - 1))
                            sPE.inc(i)
                        pe_kv = sPE.n

                        @section("vector")
                        def _(v, tc=tc, half=half, bankk=bankk, bankv=bankv,
                              pe_kv=pe_kv):
                            v.wait_ge(sPE.ap, pe_kv)
                            v.tensor_tensor(
                                ktok[:, tc * D + half * 512:tc * D + half * 512 + 512],
                                bankk[:, 0:512],
                                G5[:, half * 512:half * 512 + 512], ALU.add)
                            sDVE.inc(v.tensor_tensor(
                                vtok[:, tc * D + half * 512:tc * D + half * 512 + 512],
                                bankv[:, 0:512],
                                G6[:, half * 512:half * 512 + 512], ALU.add))
                        dve_ev = sDVE.n
                        dve_kvev[half] = dve_ev

                        @section("scalar")
                        def _(a, tc=tc, half=half, dve_ev=dve_ev):
                            a.wait_ge(sDVE.ap, dve_ev)
                            sACT.inc(a.activation(
                                ktok[:, tc * D + half * 512:tc * D + half * 512 + 512],
                                ktok[:, tc * D + half * 512:tc * D + half * 512 + 512],
                                AF.Relu))
                    act_relu = sACT.n
                    dve_base = sDVE.n
                    for h in range(H):
                        bank = pb[6 + (h % 2)]

                        @section("tensor")
                        def _(t, tc=tc, h=h, bank=bank, act_relu=act_relu,
                              dve_base=dve_base):
                            t.wait_ge(sACT.ap, act_relu)
                            if h >= 2:
                                t.wait_ge(sDVE.ap, dve_base + (h - 1))
                            i = t.matmul(bank[0:64, 0:64],
                                         ktok[:, tc * D + h * 64:tc * D + h * 64 + 64],
                                         vtok[:, tc * D + h * 64:tc * D + h * 64 + 64],
                                         start=True, stop=True)
                            sPE.inc(i)
                        pe_h = sPE.n

                        @section("vector")
                        def _(v, h=h, bank=bank, pe_h=pe_h):
                            v.wait_ge(sPE.ap, pe_h)
                            sDVE.inc(v.tensor_tensor(
                                G4[0:64, 4096 + h * 64:4096 + h * 64 + 64],
                                G4[0:64, 4096 + h * 64:4096 + h * 64 + 64],
                                bank[0:64, 0:64], ALU.add))
                if ti + 1 < NT1:
                    ds_ln = emit_ln_stats((ti + 1) * T1, T1, dve_free=dve_ln)
            dve_all = sDVE.n

            @section("scalar")
            def _(a):
                a.wait_ge(sDVE.ap, dve_all)
                sACT.inc(a.activation(kvacc, kvacc, AF.Copy,
                                      scale=1.0 / float(L * DH)))
            act_kvs = sACT.n

            @section("sync")
            def _(q):
                q.wait_ge(sACT.ap, act_kvs)
                sLD.inc(q.dma_start(out=G4[64:128, 4096:5120],
                                    in_=G4[0:64, 4096:5120]))
            if DEBUG:
                act_kv = sACT.n

                @section("sync")
                def _(q):
                    q.wait_ge(sACT.ap, act_kv)
                    sST.inc(q.dma_start(out=dbg["dbg_kv"], in_=kvacc))
            barrier()

        def phase_attn0(p):
            Wq, Wo = G1, G2
            qT = G3
            attnT = G4[:, 0:4096]
            ds_ln = None

            @section("sync")
            def _(q):
                sLD.inc(q.dma_start(out=Wq.rearrange("p (c n) -> p c n", c=NCH),
                                    in_=dram["l0_Wq"].rearrange("(c p) n -> p c n", p=128)))
                sLD.inc(q.dma_start(out=Wo.rearrange("p (c n) -> p c n", c=NCH),
                                    in_=dram["l0_Wo"].rearrange("(c p) n -> p c n", p=128)))
            ld_w = sLD.n
            ds_ln = emit_ln_stats(0, T1)
            for ti in range(NT1):
                t0 = ti * T1
                emit_ln_apply(t0, T1, ds_ln)
                dve_ln = sDVE.n
                act_qd = {}
                for hp in range(8):
                    bank = pb[2 + (hp % 2)]

                    @section("tensor")
                    def _(t, hp=hp, bank=bank, dve_ln=dve_ln):
                        t.wait_ge(sLD.ap, ld_w)
                        t.wait_ge(sDVE.ap, dve_ln)
                        if hp >= 2:
                            t.wait_ge(sACT.ap, act_qd[hp - 2])
                        for c in range(NCH):
                            i = t.matmul(
                                bank[:, 0:T1],
                                Wq[:, c * D + hp * 128:c * D + hp * 128 + 128],
                                xln[:, c * T1:(c + 1) * T1],
                                start=(c == 0), stop=(c == NCH - 1))
                        sPE.inc(i)
                    pe_q = sPE.n

                    @section("scalar")
                    def _(a, hp=hp, bank=bank, pe_q=pe_q):
                        a.wait_ge(sPE.ap, pe_q)
                        sACT.inc(a.activation(qT[:, hp * T1:(hp + 1) * T1],
                                              bank[:, 0:T1], AF.Relu,
                                              bias=bq[:, hp:hp + 1]))
                    act_qd[hp] = sACT.n
                act_q = sACT.n
                for h in range(H):
                    bank = pb[4 + (h % 2)]

                    @section("tensor")
                    def _(t, h=h, bank=bank, act_q=act_q):
                        t.wait_ge(sACT.ap, act_q + (h - 1 if h >= 2 else 0))
                        base = (h % 2) * 64
                        i = t.matmul(bank[base:base + 64, 0:T1],
                                     G4[base:base + 64,
                                        4096 + h * 64:4096 + h * 64 + 64],
                                     qT[base:base + 64,
                                        (h // 2) * T1:(h // 2 + 1) * T1],
                                     start=True, stop=True,
                                     tile_position=(base, base))
                        sPE.inc(i)
                    pe_a = sPE.n

                    @section("scalar")
                    def _(a, h=h, bank=bank, pe_a=pe_a):
                        a.wait_ge(sPE.ap, pe_a)
                        base = (h % 2) * 64
                        sACT.inc(a.activation(
                            attnT[base:base + 64,
                                  (h // 2) * T1:(h // 2 + 1) * T1],
                            bank[base:base + 64, 0:T1], AF.Copy))
                act_attn = sACT.n
                dve_wo = {}
                for oc in range(NCH):
                    bank = pb[6 + (oc % 2)]

                    @section("tensor")
                    def _(t, oc=oc, bank=bank, act_attn=act_attn):
                        if oc == 0:
                            t.wait_ge(sACT.ap, act_attn)
                        if oc >= 2:
                            t.wait_ge(sDVE.ap, dve_wo[oc - 2])
                        for c in range(NCH):
                            i = t.matmul(
                                bank[:, 0:T1],
                                Wo[:, c * D + oc * 128:c * D + oc * 128 + 128],
                                attnT[:, c * T1:(c + 1) * T1],
                                start=(c == 0), stop=(c == NCH - 1))
                        sPE.inc(i)
                    pe_o = sPE.n

                    @section("vector")
                    def _(v, oc=oc, bank=bank, pe_o=pe_o, t0=t0):
                        v.wait_ge(sPE.ap, pe_o)
                        sDVE.inc(v.scalar_tensor_tensor(
                            xres[:, oc * L + t0:oc * L + t0 + T1],
                            bank[:, 0:T1], bo[:, oc:oc + 1],
                            xres[:, oc * L + t0:oc * L + t0 + T1],
                            ALU.add, ALU.add))
                    dve_wo[oc] = sDVE.n
                if ti + 1 < NT1:
                    ds_ln = emit_ln_stats((ti + 1) * T1, T1, dve_free=dve_ln)
            if DEBUG:
                dve_x1 = sDVE.n

                @section("sync")
                def _(q):
                    q.wait_ge(sDVE.ap, dve_x1)
                    for c in range(NCH):
                        sST.inc(q.dma_start(
                            out=dbg["dbg_x1"][c * 128:(c + 1) * 128, :],
                            in_=xres[:, c * L:(c + 1) * L]))
            barrier()

        def _cross_stage_dve(bufA, bufB, si, arr_s, arr_p, stage, slabw,
                             pe_wait=None, ld_wait=None):
            d = 2 ** (stage - 7)

            @section("vector")
            def _(v):
                v.wait_ge(sDVE.ap, sDVE.n)
                if pe_wait:
                    v.wait_ge(sPE.ap, pe_wait)
                if ld_wait:
                    v.wait_ge(sLD.ap, ld_wait)
                for ch in range(32):
                    i = v.tensor_scalar(
                        bufB[:, ch * slabw:(ch + 1) * slabw],
                        bufA[:, (ch ^ d) * slabw:((ch ^ d) + 1) * slabw],
                        coef[:, arr_p * 160 + si * 32 + ch:
                             arr_p * 160 + si * 32 + ch + 1],
                        None, ALU.mult)
                sDVE.inc(i)
                v.wait_ge(sDVE.ap, sDVE.n)
                for ch in range(32):
                    i = v.scalar_tensor_tensor(
                        bufB[:, ch * slabw:(ch + 1) * slabw],
                        bufA[:, ch * slabw:(ch + 1) * slabw],
                        coef[:, arr_s * 160 + si * 32 + ch:
                             arr_s * 160 + si * 32 + ch + 1],
                        bufB[:, ch * slabw:(ch + 1) * slabw],
                        ALU.mult, ALU.add)
                sDVE.inc(i)

        def phase_fold():
            coef_base = 1280
            smats = [G4[:, 0:896], G4[:, 1920:2816]]
            vpp = [G4[:, 896:1024], G4[:, 1024:1152]]
            ttmps = [G4[:, 1152:1280], G4[:, 2816:2944],
                     G4[:, 2944:3072], G4[:, 3072:3200]]
            devac = [G4[:, 3200:3456], G4[:, 3456:3712]]
            Acmp = G3

            @section("sync")
            def _(q):
                for i, nm in enumerate(["bfu_self", "bfu_part",
                                        "bfd_self", "bfd_part"]):
                    sLD.inc(q.dma_start(
                        out=G4[:, coef_base + i * 160:coef_base + (i + 1) * 160],
                        in_=dram[nm]))
            ld_coef = sLD.n

            for updown in range(2):
                mats_nm = "bfu_mats" if updown == 0 else "bfd_mats"
                pe_blk = {}
                for bp in range(16):
                    pair = (2 * bp, 2 * bp + 1)
                    ld_ms = {}
                    for w2, b in enumerate(pair):
                        smat = smats[w2]

                        @section("sync")
                        def _(q, b=b, w2=w2, smat=smat, bp=bp):
                            if bp >= 1:
                                q.wait_ge(sPE.ap, pe_blk[b - 2])
                            elif updown == 1:
                                q.wait_ge(sPE.ap, sPE.n)
                            sWB[w2].inc(q.dma_start(
                                out=smat.rearrange("p (s c) -> p s c", s=7),
                                in_=dram[mats_nm][b]
                                    .rearrange("s p c -> p s c")))
                        ld_ms[w2] = (sWB[w2], sWB[w2].n)

                    dve_step = {}
                    @section("vector")
                    def _(v, ld_ms=ld_ms, bp=bp):
                        for w2 in range(2):
                            v.wait_ge(ld_ms[w2][0].ap, ld_ms[w2][1])
                            if bp >= 1:
                                v.wait_ge(sPE.ap, pe_blk[2 * bp - 2 + w2])
                            sDVE.inc(v.tensor_copy(vpp[w2],
                                                   smats[w2][:, 0:128]))
                    dve_step[(0, 0)] = sDVE.n - 1
                    dve_step[(1, 0)] = sDVE.n

                    pe_step = {}
                    for j in range(1, 7):
                        for w2, b in enumerate(pair):
                            bank = pb[w2 * 2 + (j % 2)]

                            @section("tensor")
                            def _(t, j=j, w2=w2, bank=bank, ld_ms=ld_ms):
                                t.wait_ge(ld_ms[w2][0].ap, ld_ms[w2][1])
                                t.wait_ge(sDVE.ap, dve_step[(w2, j - 1)])
                                sPE.inc(t.matmul(
                                    bank[:, 0:128],
                                    smats[w2][:, j * 128:(j + 1) * 128],
                                    vpp[w2], start=True, stop=True))
                            pe_step[(w2, j)] = sPE.n
                        for w2, b in enumerate(pair):
                            bank = pb[w2 * 2 + (j % 2)]

                            @section("vector")
                            def _(v, j=j, w2=w2, b=b, bank=bank):
                                v.wait_ge(sPE.ap, pe_step[(w2, j)])
                                if j < 6:
                                    sDVE.inc(v.tensor_copy(vpp[w2],
                                                           bank[:, 0:128]))
                                else:
                                    sDVE.inc(v.tensor_copy(
                                        Acmp[:, b * 128:(b + 1) * 128],
                                        bank[:, 0:128]))
                            dve_step[(w2, j)] = sDVE.n
                    for w2, b in enumerate(pair):
                        pe_blk[b] = pe_step[(w2, 6)]
                wsrc = dram["l0_WexpT"] if updown == 0 else dram["l0_Wcon"]
                wdst = wexp_f if updown == 0 else wcon_f
                nslab = 5 if updown == 0 else 4
                for slab in range(nslab):
                    slabw = 1 if (updown == 0 and slab == 4) else 256
                    c0 = slab * 256

                    sstr = 256 if slabw == 256 else 1

                    @section("sync")
                    def _(q, c0=c0, slabw=slabw):
                        q.wait_ge(sPE.ap, sPE.n)
                        q.wait_ge(sDVE.ap, sDVE.n)
                        if slabw == 256:
                            sLD.inc(q.dma_start(
                                out=G1.rearrange("p (b n) -> p b n", n=256),
                                in_=wsrc.rearrange("(b p) n -> p b n", p=128)
                                    [:, :, c0:c0 + 256]))
                        else:
                            sLD.inc(q.dma_start(out=G1[:, 0:32],
                                                in_=dram["l0_bexp_pp"]))
                    ld_s = sLD.n

                    if updown == 0:
                        for b in range(32):
                            bank = pb[b % 2]

                            @section("tensor")
                            def _(t, b=b, bank=bank, ld_s=ld_s, slabw=slabw,
                                  sstr=sstr):
                                t.wait_ge(sLD.ap, ld_s)
                                t.wait_ge(sDVE.ap, sDVE.n)
                                sPE.inc(t.matmul(
                                    bank[:, 0:slabw],
                                    Acmp[:, b * 128:(b + 1) * 128],
                                    G1[:, b * sstr:b * sstr + slabw],
                                    start=True, stop=True))
                            pe_b = sPE.n

                            @section("vector")
                            def _(v, b=b, bank=bank, pe_b=pe_b, slabw=slabw,
                                  sstr=sstr):
                                v.wait_ge(sPE.ap, pe_b)
                                sDVE.inc(v.tensor_copy(
                                    G2[:, b * sstr:b * sstr + slabw],
                                    bank[:, 0:slabw]))
                        cur, oth = G2, G1
                        pe_bf = sPE.n
                        for si in range(5):
                            _cross_stage_dve(cur, oth, si, 0, 1, 7 + si,
                                             256 if slabw == 256 else 1,
                                             pe_wait=pe_bf, ld_wait=ld_s)
                            cur, oth = oth, cur
                        final = cur
                        dve_f = sDVE.n
                        if slabw == 1:
                            @section("vector")
                            def _(v, final=final, dve_f=dve_f):
                                v.wait_ge(sDVE.ap, dve_f)
                                sDVE.inc(v.tensor_tensor(
                                    btot, final[:, 0:32], upb, ALU.add))
                        else:
                            dve_tpf = {}
                            for b in range(32):
                                for c2 in range(2):
                                    ti2 = b * 2 + c2
                                    bank = pb[2 + (ti2 % 2)]
                                    ttmp = ttmps[(ti2 % 2) * 2]
                                    tbc = ttmps[(ti2 % 2) * 2 + 1].bitcast(BF16)
                                    wc = sWC[ti2 % 2]

                                    @section("tensor")
                                    def _(t, b=b, c2=c2, bank=bank,
                                          dve_f=dve_f, final=final,
                                          ti2=ti2):
                                        if ti2 == 0:
                                            t.wait_ge(sDVE.ap, dve_f)
                                        if ti2 >= 2:
                                            t.wait_ge(sDVE.ap, dve_tpf[ti2 - 2])
                                        sPE.inc(t.transpose(
                                            bank[:, 0:128],
                                            final[:, b * 256 + c2 * 128:
                                                  b * 256 + c2 * 128 + 128],
                                            ident))
                                    pe_t = sPE.n

                                    @section("vector")
                                    def _(v, bank=bank, pe_t=pe_t, ti2=ti2,
                                          ttmp=ttmp, tbc=tbc, wc=wc):
                                        v.wait_ge(sPE.ap, pe_t)
                                        if wc.n > 0:
                                            v.wait_ge(wc.ap, wc.n)
                                        sDVE.inc(v.tensor_copy(ttmp,
                                                               bank[:, 0:128]))
                                        v.wait_ge(sDVE.ap, sDVE.n)
                                        sDVE.inc(v.tensor_copy(tbc[:, 0:128],
                                                               ttmp))
                                        v.wait_ge(sDVE.ap, sDVE.n)
                                        sDVE.inc(v.tensor_tensor(
                                            ttmp, ttmp, tbc[:, 0:128],
                                            ALU.subtract))
                                        v.wait_ge(sDVE.ap, sDVE.n)
                                        sDVE.inc(v.tensor_copy(
                                            tbc[:, 128:256], ttmp))
                                    dve_t2 = sDVE.n
                                    dve_tpf[ti2] = dve_t2

                                    @section("sync")
                                    def _(q, b=b, c2=c2, c0=c0, dve_t2=dve_t2,
                                          tbc=tbc, wc=wc):
                                        q.wait_ge(sDVE.ap, dve_t2)
                                        wc.inc(q.dma_start(
                                            out=wexp_h[c0 + c2 * 128:
                                                       c0 + c2 * 128 + 128,
                                                       b * 128:(b + 1) * 128],
                                            in_=tbc[:, 0:128]))
                                        wc.inc(q.dma_start(
                                            out=wexp_l[c0 + c2 * 128:
                                                       c0 + c2 * 128 + 128,
                                                       b * 128:(b + 1) * 128],
                                            in_=tbc[:, 128:256]))
                    else:
                        cur, oth = G1, G2
                        pe_bf = sPE.n
                        for si in [4, 3, 2, 1, 0]:
                            _cross_stage_dve(cur, oth, si, 2, 3, 7 + si, 256,
                                             pe_wait=pe_bf, ld_wait=ld_s)
                            cur, oth = oth, cur
                        dve_f = sDVE.n
                        dve_bdf = {}
                        for b in range(32):
                            bank = pb[b % 2]
                            dv = devac[b % 2]
                            wc = sWC[b % 2]

                            @section("tensor")
                            def _(t, b=b, bank=bank, dve_f=dve_f, cur=cur):
                                if b == 0:
                                    t.wait_ge(sDVE.ap, dve_f)
                                if b >= 2:
                                    t.wait_ge(sDVE.ap, dve_bdf[b - 2])
                                sPE.inc(t.matmul(
                                    bank[:, 0:256],
                                    Acmp[:, b * 128:(b + 1) * 128],
                                    cur[:, b * 256:(b + 1) * 256],
                                    start=True, stop=True))
                            pe_b = sPE.n

                            @section("vector")
                            def _(v, b=b, bank=bank, pe_b=pe_b, dv=dv, wc=wc):
                                v.wait_ge(sPE.ap, pe_b)
                                if wc.n > 0:
                                    v.wait_ge(wc.ap, wc.n)
                                sDVE.inc(v.tensor_copy(dv, bank[:, 0:256]))
                            dve_e = sDVE.n
                            dve_bdf[b] = dve_e

                            @section("sync")
                            def _(q, b=b, c0=c0, dve_e=dve_e, dv=dv, wc=wc):
                                q.wait_ge(sDVE.ap, dve_e)
                                wc.inc(q.dma_start(
                                    out=wdst[b * 128:(b + 1) * 128,
                                             c0:c0 + 256],
                                    in_=dv))
            if DEBUG:
                @section("sync")
                def _(q):
                    q.wait_ge(sST.ap, sST.n)
                    for wc in sWC:
                        if wc.n > 0:
                            q.wait_ge(wc.ap, wc.n)
                    for c in range(32):
                        sST.inc(q.dma_start(
                            out=dbg["dbg_wconf"][c * 128:(c + 1) * 128, :],
                            in_=wcon_f[c * 128:(c + 1) * 128, :]))
                    sST.inc(q.dma_start(out=dbg["dbg_bexp"], in_=btot))
            barrier()

        def phase_ffn0(p):
            ds_ln = emit_ln_stats(0, T1)
            pe_g4 = {}
            xbc = xln.bitcast(BF16)
            stg = [G4[:, 4096:4608], G4[:, 4608:5120]]
            for ti in range(NT1):
                t0 = ti * T1
                emit_ln_apply(t0, T1, ds_ln)
                @section("vector")
                def _(v):
                    v.wait_ge(sDVE.ap, sDVE.n)
                    for c in range(NCH):
                        sg = stg[c % 2]
                        if c >= 2:
                            v.wait_ge(sDVE.ap, sDVE.n)
                        sDVE.inc(v.tensor_copy(sg, xln[:, c * T1:(c + 1) * T1]))
                        v.wait_ge(sDVE.ap, sDVE.n)
                        sDVE.inc(v.tensor_copy(
                            xbc[:, c * 1024:c * 1024 + 512], sg))
                        v.wait_ge(sDVE.ap, sDVE.n)
                        sDVE.inc(v.tensor_tensor(
                            sg, sg, xbc[:, c * 1024:c * 1024 + 512],
                            ALU.subtract))
                        v.wait_ge(sDVE.ap, sDVE.n)
                        sDVE.inc(v.tensor_copy(
                            xbc[:, c * 1024 + 512:(c + 1) * 1024], sg))
                dve_ln = sDVE.n
                act_base = sACT.n
                pe_fcg = {}
                for fcg in range(16):
                    if fcg == 0:
                        wbuf = G4[:, 0:2048]
                    elif fcg == 1:
                        wbuf = G4[:, 2048:4096]
                    else:
                        wbuf = G5 if fcg % 2 == 0 else G6

                    wsem = sWC[2 + fcg] if fcg < 2 else sWB[fcg % 2]
                    wbc = wbuf[:, 0:2048].bitcast(BF16)

                    @section("sync")
                    def _(q, ti=ti, fcg=fcg, wbuf=wbuf, wsem=wsem, wbc=wbc):
                        if fcg < 2:
                            q.wait_ge(sPE.ap, pe_g4.get((ti - 1, fcg), 0)
                                      if ti > 0 else sPE.n)
                        elif fcg < 4:
                            q.wait_ge(sPE.ap, sPE.n)
                        else:
                            q.wait_ge(sPE.ap, pe_fcg[fcg - 2])
                        wsem.inc(q.dma_start(
                            out=wbc[:, 0:2048].rearrange("p (c n) -> p c n",
                                                         c=NCH),
                            in_=wexp_h.rearrange("(c p) n -> p c n", p=128)
                                [:, :, fcg * 256:(fcg + 1) * 256]))
                        wsem.inc(q.dma_start(
                            out=wbc[:, 2048:4096].rearrange("p (c n) -> p c n",
                                                            c=NCH),
                            in_=wexp_l.rearrange("(c p) n -> p c n", p=128)
                                [:, :, fcg * 256:(fcg + 1) * 256]))
                    ld_w = (wsem, wsem.n)

                    for f2 in range(2):
                        fc = fcg * 2 + f2
                        bank = pb[2 + (fc % 2)]
                        h3buf = G1 if fc < 16 else G2
                        hcol = (fc % 16) * T1

                        wbc = wbuf[:, 0:2048].bitcast(BF16)

                        @section("tensor")
                        def _(t, fc=fc, f2=f2, bank=bank, wbc=wbc,
                              ld_w=ld_w, dve_ln=dve_ln, ab=act_base,
                              hb=None):
                            t.wait_ge(ld_w[0].ap, ld_w[1])
                            if fc == 0:
                                t.wait_ge(sDVE.ap, dve_ln)
                            if fc >= 2:
                                t.wait_ge(sACT.ap, ab + fc - 1)
                            first = True
                            for c in range(NCH):
                                whi = wbc[:, c * 256 + f2 * 128:
                                          c * 256 + f2 * 128 + 128]
                                wlo = wbc[:, 2048 + c * 256 + f2 * 128:
                                          2048 + c * 256 + f2 * 128 + 128]
                                xhi = xbc[:, c * 1024:c * 1024 + 512]
                                xlo = xbc[:, c * 1024 + 512:(c + 1) * 1024]
                                for wop, xop in ((whi, xhi), (whi, xlo),
                                                 (wlo, xhi)):
                                    i = t.matmul(
                                        bank[:, 0:T1], wop, xop,
                                        start=first,
                                        stop=(c == NCH - 1 and wop is wlo))
                                    first = False
                            sPE.inc(i)
                        pe_h = sPE.n
                        pe_fcg[fcg] = pe_h
                        if fcg < 2:
                            pe_g4[(ti, fcg)] = pe_h

                        @section("scalar")
                        def _(a, fc=fc, bank=bank, pe_h=pe_h, h3buf=h3buf,
                              hcol=hcol):
                            a.wait_ge(sPE.ap, pe_h)
                            sACT.inc(a.activation(
                                h3buf[:, hcol:hcol + T1], bank[:, 0:T1],
                                AF.Gelu, bias=btot[:, fc:fc + 1]))
                act_h = sACT.n
                pe_yod = {}
                dve_yo = {}
                for oc in range(NCH):
                    for half in range(2):
                        wbuf = G5 if half == 0 else G6
                        ykey = oc * 2 + half

                        @section("sync")
                        def _(q, oc=oc, half=half, wbuf=wbuf, ykey=ykey):
                            if ykey >= 2:
                                q.wait_ge(sPE.ap, pe_yod[ykey - 2])
                            else:
                                q.wait_ge(sPE.ap, pe_fcg[14 + half])
                            sWB[half].inc(q.dma_start(
                                out=wbuf[:, 0:2048].rearrange(
                                    "p (f n) -> p f n", f=16),
                                in_=wcon_f.rearrange("(f p) n -> p f n", p=128)
                                    [:, half * 16:(half + 1) * 16,
                                     oc * 128:(oc + 1) * 128]))
                        ld_w2 = (sWB[half], sWB[half].n)

                        bank = pb[4 + (oc % 2)]

                        @section("tensor")
                        def _(t, oc=oc, half=half, wbuf=wbuf, bank=bank,
                              ld_w2=ld_w2, act_h=act_h):
                            t.wait_ge(ld_w2[0].ap, ld_w2[1])
                            if oc == 0 and half == 0:
                                t.wait_ge(sACT.ap, act_h)
                            if half == 0 and oc >= 2:
                                t.wait_ge(sDVE.ap, dve_yo[oc - 2])
                            elif half == 0:
                                t.wait_ge(sDVE.ap, sDVE.n)
                            for f in range(16):
                                fc = half * 16 + f
                                h3buf = G1 if fc < 16 else G2
                                i = t.matmul(
                                    bank[:, 0:T1],
                                    wbuf[:, f * 128:(f + 1) * 128],
                                    h3buf[:, (fc % 16) * T1:
                                          (fc % 16) * T1 + T1],
                                    start=(fc == 0), stop=(fc == 31))
                            sPE.inc(i)
                        pe_yod[oc * 2 + half] = sPE.n
                    pe_y = sPE.n

                    @section("vector")
                    def _(v, oc=oc, pe_y=pe_y, t0=t0):
                        bank = pb[4 + (oc % 2)]
                        v.wait_ge(sPE.ap, pe_y)
                        sDVE.inc(v.scalar_tensor_tensor(
                            xres[:, oc * L + t0:oc * L + t0 + T1],
                            bank[:, 0:T1], bcd[:, oc:oc + 1],
                            xres[:, oc * L + t0:oc * L + t0 + T1],
                            ALU.add, ALU.add))
                    dve_yo[oc] = sDVE.n
                if ti + 1 < NT1:
                    ds_ln = emit_ln_stats((ti + 1) * T1, T1, dve_free=dve_ln)
            if DEBUG:
                dve_x2 = sDVE.n

                @section("sync")
                def _(q):
                    q.wait_ge(sDVE.ap, dve_x2)
                    for c in range(NCH):
                        sST.inc(q.dma_start(
                            out=dbg["dbg_x2"][c * 128:(c + 1) * 128, :],
                            in_=xres[:, c * L:(c + 1) * L]))
            barrier()

        def phase_attn1(p):
            TG = 256
            SW = 512
            Sstrip = G1
            attnF = lambda j: xln[:, j * 512 + 256:(j + 1) * 512]
            projTbuf = [G5[:, 0:1024], G5[:, 1024:2048]]
            q_tok = G2[:, 0:2048]
            k_tok = G2[:, 2048:4096]
            qTf = G2[0:64, 4096:6144]
            kTf = G2[:, 6144:7168]
            qscr = G3[0:64, 0:2048]
            OT_odd = G3[64:128, 0:2048]
            Vaug = [G4[:, 0:1040], G4[:, 1040:2080]]
            OT = G4[0:64, 2080:4128]
            qscr_hi = G4[64:128, 2080:4128]
            wsl = [G5[:, 0:1024], G5[:, 1024:2048]]
            maskB = G6[:, 0:2080]

            @section("sync")
            def _(q):
                sOO.inc(q.dma_start(out=maskB, in_=dram["l1_maskbig"]))
            ld_mask = sOO.n

            ds_ln = emit_ln_stats(0, TG)
            pe_g5prev = {}
            for grp in range(L // TG):
                t0 = grp * TG
                emit_ln_apply(t0, TG, ds_ln)
                dve_ln = sDVE.n
                pe_wslice = {}
                for wi, wnm in enumerate(["l1_Wq", "l1_Wk", "l1_Wv"]):
                    for c in range(NCH):
                        wb = wsl[c % 2]
                        wkey = wi * NCH + c

                        @section("sync")
                        def _(q, wi=wi, wnm=wnm, c=c, wb=wb, wkey=wkey,
                              grp=grp):
                            if wkey >= 2:
                                q.wait_ge(sPE.ap, pe_wslice[wkey - 2])
                            elif grp > 0:
                                q.wait_ge(sPE.ap, pe_g5prev[wkey])
                            else:
                                q.wait_ge(sPE.ap, sPE.n)
                            sWB[wkey % 2].inc(q.dma_start(
                                out=wb, in_=dram[wnm][c * 128:(c + 1) * 128, :]))
                        ld_w = (sWB[wkey % 2], sWB[wkey % 2].n)

                        @section("tensor")
                        def _(t, wi=wi, c=c, wb=wb, ld_w=ld_w, dve_ln=dve_ln):
                            t.wait_ge(ld_w[0].ap, ld_w[1])
                            if c == 0:
                                t.wait_ge(sDVE.ap, dve_ln)
                                t.wait_ge(sACT.ap, sACT.n)
                            for u in range(2):
                                for half in range(2):
                                    i = t.matmul(
                                        pb[2 + u * 2 + half][:, 0:512],
                                        xln[:, c * T1 + u * 128:
                                            c * T1 + u * 128 + 128],
                                        wb[:, half * 512:half * 512 + 512],
                                        start=(c == 0), stop=(c == NCH - 1))
                            sPE.inc(i)
                        pe_wslice[wi * NCH + c] = sPE.n
                    pe_w = sPE.n

                    @section("scalar")
                    def _(a, wi=wi, pe_w=pe_w):
                        a.wait_ge(sPE.ap, pe_w)
                        for u in range(2):
                            for half in range(2):
                                src = pb[2 + u * 2 + half][:, 0:512]
                                if wi == 0:
                                    i = a.activation(
                                        q_tok[:, u * 1024 + half * 512:
                                              u * 1024 + half * 512 + 512],
                                        src, AF.Copy, scale=0.125)
                                elif wi == 1:
                                    i = a.activation(
                                        k_tok[:, u * 1024 + half * 512:
                                              u * 1024 + half * 512 + 512],
                                        src, AF.Copy)
                                else:
                                    i = a.activation(
                                        Vaug[u][:, half * 8 * 65:]
                                        .rearrange("p (h r) -> p h r", r=65)
                                        [:, 0:8, 0:64],
                                        src.rearrange("p (h r) -> p h r", r=64),
                                        AF.Copy)
                            sACT.inc(i)
                act_qkv = sACT.n

                @section("vector")
                def _(v, act_qkv=act_qkv):
                    v.wait_ge(sACT.ap, act_qkv)
                    for u in range(2):
                        i = v.memset(Vaug[u].rearrange("p (h r) -> p h r", r=65)
                                     [:, 0:16, 64:65], 1.0)
                    sDVE.inc(i)
                dve_vones = sDVE.n

                for u in range(2):
                    ubase = grp * 2 + u
                    dve_tp = {}
                    for wi in range(2):
                        src = q_tok if wi == 0 else k_tok
                        for i16 in range(16):
                            tidx = wi * 16 + i16
                            odd = (wi == 1 and i16 % 2 == 1)
                            bank = pb[6 + (i16 % 2)]

                            @section("tensor")
                            def _(t, u=u, wi=wi, i16=i16, bank=bank,
                                  act_qkv=act_qkv, src=src, odd=odd,
                                  tidx=tidx):
                                if tidx == 0:
                                    t.wait_ge(sACT.ap, act_qkv)
                                if tidx < 2:
                                    t.wait_ge(sDVE.ap, sDVE.n)
                                else:
                                    t.wait_ge(sDVE.ap, dve_tp[tidx - 2])
                                if odd:
                                    sPE.inc(t.matmul(
                                        bank[64:128, 0:128],
                                        src[:, u * 1024 + i16 * 64:
                                            u * 1024 + i16 * 64 + 64],
                                        ident, start=True, stop=True,
                                        tile_position=(0, 64)))
                                else:
                                    sPE.inc(t.transpose(
                                        bank[0:64, 0:128],
                                        src[:, u * 1024 + i16 * 64:
                                            u * 1024 + i16 * 64 + 64],
                                        ident))
                            pe_t = sPE.n

                            @section("vector")
                            def _(v, wi=wi, i16=i16, bank=bank, pe_t=pe_t,
                                  odd=odd):
                                v.wait_ge(sPE.ap, pe_t)
                                if wi == 0:
                                    sDVE.inc(v.tensor_copy(
                                        qTf[:, i16 * 128:(i16 + 1) * 128],
                                        bank[0:64, 0:128]))
                                elif not odd:
                                    sDVE.inc(v.tensor_copy(
                                        kTf[0:64, (i16 // 2) * 128:
                                            (i16 // 2) * 128 + 128],
                                        bank[0:64, 0:128]))
                                else:
                                    sDVE.inc(v.tensor_copy(
                                        kTf[64:128, (i16 // 2) * 128:
                                            (i16 // 2) * 128 + 128],
                                        bank[64:128, 0:128]))
                            dve_tp[tidx] = sDVE.n
                    dve_tf = sDVE.n

                    @section("vector")
                    def _(v, dve_tf=dve_tf):
                        v.wait_ge(sDVE.ap, dve_tf)
                        if sOO.n > 0:
                            v.wait_ge(sOO.ap, sOO.n)
                        for i16 in range(16):
                            i = v.tensor_copy(
                                qscr.rearrange("p (t h) -> p t h", h=16)
                                [:, 0:128, i16:i16 + 1],
                                qTf[:, i16 * 128:(i16 + 1) * 128]
                                .rearrange("p (t o) -> p t o", o=1))
                        sDVE.inc(i)
                    dve_scr0 = sDVE.n

                    @section("sync")
                    def _(q, dve_scr0=dve_scr0):
                        q.wait_ge(sDVE.ap, dve_scr0)
                        sOO.inc(q.dma_start(out=qscr_hi, in_=qscr))
                    ld_qh = sOO.n

                    dve_scr = dve_scr0

                    pending_tail = []
                    for st8 in range(2048 // SW):
                        PV = min(128, 32 * st8 + 32)
                        pe_sbase = sPE.n
                        dve_sbase = sDVE.n
                        for kp in range(8):
                            bankE = pb[2 + (kp % 2) * 4]
                            bankO = pb[3 + (kp % 2) * 4]

                            @section("tensor")
                            def _(t, st8=st8, kp=kp, bankE=bankE, bankO=bankO,
                                  dve_scr=dve_scr, dve_sbase=dve_sbase,
                                  ld_qh=ld_qh, PV=PV):
                                t.wait_ge(sDVE.ap, dve_scr)
                                if kp == 0:
                                    t.wait_ge(sOO.ap, ld_qh)
                                if kp >= 2:
                                    t.wait_ge(sDVE.ap, dve_sbase + kp - 1)
                                sPE.inc(t.matmul(
                                    bankE[0:PV, 0:SW],
                                    kTf[0:64, kp * 128:kp * 128 + PV],
                                    qscr[:, st8 * SW:(st8 + 1) * SW],
                                    start=True, stop=True))
                                sPE.inc(t.matmul(
                                    bankO[0:PV, 0:SW],
                                    kTf[64:128, kp * 128:kp * 128 + PV],
                                    qscr_hi[:, st8 * SW:(st8 + 1) * SW],
                                    start=True, stop=True,
                                    tile_position=(64, 0)))
                            pe_s = sPE.n

                            @section("vector")
                            def _(v, st8=st8, kp=kp, bankE=bankE, bankO=bankO,
                                  pe_s=pe_s, ld_mask=ld_mask, PV=PV):
                                v.wait_ge(sPE.ap, pe_s)
                                if kp == 0:
                                    v.wait_ge(sOO.ap, ld_mask)
                                for kc, bank in ((2 * kp, bankE),
                                                 (2 * kp + 1, bankO)):
                                    m0 = st8 * SW + 15 - kc
                                    i = v.scalar_tensor_tensor(
                                        Sstrip[0:PV, kc * SW:(kc + 1) * SW],
                                        bank[0:PV, 0:SW], 80.0,
                                        maskB[0:PV, m0:m0 + SW],
                                        ALU.min, ALU.add)
                                sDVE.inc(i)
                        dve_smm = sDVE.n
                        dve_sbase2 = dve_sbase
                        if pending_tail:
                            pending_tail.pop(0)()

                        act_eb = sACT.n

                        @section("scalar")
                        def _(a, dve_sbase2=dve_sbase2, PV=PV):
                            for kc in range(16):
                                a.wait_ge(sDVE.ap, dve_sbase2 + kc // 2 + 1)
                                sACT.inc(a.activation(
                                    Sstrip[0:PV, kc * SW:(kc + 1) * SW],
                                    Sstrip[0:PV, kc * SW:(kc + 1) * SW],
                                    AF.Exp))

                        @section("tensor")
                        def _(t, u=u, act_eb=act_eb, dve_vones=dve_vones,
                              PV=PV):
                            t.wait_ge(sDVE.ap, dve_vones)
                            t.wait_ge(sDVE.ap, sDVE.n)
                            for kc in range(16):
                                t.wait_ge(sACT.ap, act_eb + kc + 1)
                                i = t.matmul(
                                    pb[4][0:65, 0:SW],
                                    Vaug[u][0:PV, kc * 65:(kc + 1) * 65],
                                    Sstrip[0:PV, kc * SW:(kc + 1) * SW],
                                    start=(kc == 0), stop=(kc == 15))
                            sPE.inc(i)
                        pe_o = sPE.n

                        @section("vector")
                        def _(v, pe_o=pe_o):
                            v.wait_ge(sPE.ap, pe_o)
                            sDVE.inc(v.reciprocal(st_nm[0:1, 0:SW],
                                                  pb[4][64:65, 0:SW]))
                        dve_r = sDVE.n

                        def _tail(st8=st8, dve_r=dve_r):
                            @section("tensor")
                            def _(t):
                                t.wait_ge(sDVE.ap, dve_r)
                                sPE.inc(t.matmul(pb[5][0:64, 0:SW],
                                                 onesr[0:1, 0:64],
                                                 st_nm[0:1, 0:SW],
                                                 start=True, stop=True))
                            pe_r = sPE.n

                            @section("vector")
                            def _(v, pe_r=pe_r):
                                v.wait_ge(sPE.ap, pe_r)
                                sDVE.inc(v.tensor_copy(rsb[0:64, 0:SW],
                                                       pb[5][0:64, 0:SW]))
                                v.wait_ge(sDVE.ap, sDVE.n)
                                sDVE.inc(v.scalar_tensor_tensor(
                                    OT[:, st8 * SW:(st8 + 1) * SW],
                                    pb[4][0:64, 0:SW], 1.0, rsb[0:64, 0:SW],
                                    ALU.mult, ALU.mult))
                        pending_tail.append(_tail)
                    while pending_tail:
                        pending_tail.pop(0)()
                    dve_ot = sDVE.n

                    @section("sync")
                    def _(q, dve_ot=dve_ot):
                        q.wait_ge(sDVE.ap, dve_ot)
                        sOO.inc(q.dma_start(out=OT_odd, in_=OT))
                    ld_oo = sOO.n

                    @section("vector")
                    def _(v, u=u, dve_ot=dve_ot, ld_oo=ld_oo):
                        v.wait_ge(sDVE.ap, dve_ot)
                        v.wait_ge(sOO.ap, ld_oo)
                        for j in range(NCH):
                            v.tensor_copy(
                                attnF(j)[0:64, u * 128:u * 128 + 128],
                                OT.rearrange("p (t h) -> p t h", h=16)
                                [:, 0:128, 2 * j:2 * j + 1])
                            i = v.tensor_copy(
                                attnF(j)[64:128, u * 128:u * 128 + 128],
                                OT_odd.rearrange("p (t h) -> p t h", h=16)
                                [:, 0:128, 2 * j + 1:2 * j + 2])
                        sDVE.inc(i)
                dve_attnF = sDVE.n

                pe_pjd = {}
                dve_pj = {}
                for oc in range(NCH):
                    wb = projTbuf[oc % 2]

                    @section("sync")
                    def _(q, oc=oc, wb=wb):
                        if oc >= 2:
                            q.wait_ge(sPE.ap, pe_pjd[oc - 2])
                        else:
                            q.wait_ge(sPE.ap, sPE.n)
                        sWB[oc % 2].inc(q.dma_start(
                            out=wb.rearrange("p (c n) -> p c n", c=NCH),
                            in_=dram["l1_projT"]
                                .rearrange("(c p) n -> p c n", p=128)
                                [:, :, oc * 128:(oc + 1) * 128]))
                    ld_pj = (sWB[oc % 2], sWB[oc % 2].n)

                    bank = pb[6 + (oc % 2)]

                    @section("tensor")
                    def _(t, oc=oc, wb=wb, bank=bank, ld_pj=ld_pj,
                          dve_attnF=dve_attnF):
                        t.wait_ge(ld_pj[0].ap, ld_pj[1])
                        if oc == 0:
                            t.wait_ge(sDVE.ap, dve_attnF)
                        if oc >= 2:
                            t.wait_ge(sDVE.ap, dve_pj[oc - 2])
                        for c in range(NCH):
                            i = t.matmul(bank[:, 0:TG],
                                         wb[:, c * 128:(c + 1) * 128],
                                         attnF(c)[:, 0:256],
                                         start=(c == 0), stop=(c == NCH - 1))
                        sPE.inc(i)
                    pe_pj = sPE.n
                    pe_pjd[oc] = pe_pj
                    if oc >= 6:
                        pe_g5prev[oc - 6] = pe_pj

                    @section("vector")
                    def _(v, oc=oc, bank=bank, pe_pj=pe_pj, t0=t0):
                        v.wait_ge(sPE.ap, pe_pj)
                        sDVE.inc(v.scalar_tensor_tensor(
                            xres[:, oc * L + t0:oc * L + t0 + TG],
                            bank[:, 0:TG], projb[:, oc:oc + 1],
                            xres[:, oc * L + t0:oc * L + t0 + TG],
                            ALU.add, ALU.add))
                    dve_pj[oc] = sDVE.n
                if grp + 1 < L // TG:
                    ds_ln = emit_ln_stats((grp + 1) * TG, TG)
            if DEBUG:
                dve_x3 = sDVE.n

                @section("sync")
                def _(q):
                    q.wait_ge(sDVE.ap, dve_x3)
                    for c in range(NCH):
                        sST.inc(q.dma_start(
                            out=dbg["dbg_x3"][c * 128:(c + 1) * 128, :],
                            in_=xres[:, c * L:(c + 1) * L]))
            barrier()
        def phase_moe():
            h3 = [G1, G2]
            yacc = G3
            wsbc = G4[:, 0:512]
            mx = G4[:, 512:4608]
            ws2buf = G4[0:2, 4608:5120]

            @section("sync")
            def _(q):
                for j in range(8):
                    sLD.inc(q.dma_start(
                        out=a2a_in.ap()[j].rearrange("(c p) n -> p c n", p=128),
                        in_=xres.rearrange("p (c t) -> p c t", c=NCH)
                            [:, :, j * 256:(j + 1) * 256]))
                sOO.inc(q.dma_start(out=ws2buf, in_=dram["l1_ws2"]))
            ld_a2a = sLD.n

            @section("gpsimd")
            def _(g):
                g.wait_ge(sLD.ap, ld_a2a)
                from concourse.aot_env import allow_ici_all_to_all
                with allow_ici_all_to_all():
                    sGP.inc(g.collective_compute(
                        "AllToAll", ALU.bypass,
                        replica_groups=[list(range(NCORE))],
                        ins=[a2a_in.ap().opt()], outs=[a2a_out.ap().opt()]))
            gp_cc = sGP.n

            @section("sync")
            def _(q):
                q.wait_ge(sGP.ap, gp_cc)
                for sdr in range(2):
                    sLD.inc(q.dma_start(
                        out=mx.rearrange("p (c t) -> p c t", c=NCH)
                            [:, :, sdr * 256:(sdr + 1) * 256],
                        in_=a2a_out.ap()[sdr]
                            .rearrange("(c p) n -> p c n", p=128)))
            ld_mx = sLD.n

            for wch in range(1):
                @section("tensor")
                def _(t, ld_ws=sOO.n):
                    t.wait_ge(sOO.ap, ld_ws)
                    sPE.inc(t.matmul(
                        pb[6][:, 0:512], ones2, ws2buf[:, 0:512],
                        start=True, stop=True))
                pe_ws = sPE.n

                @section("vector")
                def _(v, pe_ws=pe_ws):
                    v.wait_ge(sPE.ap, pe_ws)
                    sDVE.inc(v.tensor_copy(wsbc, pb[6][:, 0:512]))
            dve_ws = sDVE.n

            emit_ln(0, T1, src=mx, sstr=T1, ld_wait=ld_mx)
            dve_ln = sDVE.n

            for hf in range(2):
                act_base = sACT.n
                pe_fcg = {}
                for fcg in range(16):
                    wbuf = G5 if fcg % 2 == 0 else G6

                    @section("sync")
                    def _(q, hf=hf, fcg=fcg, wbuf=wbuf):
                        if fcg >= 2:
                            q.wait_ge(sPE.ap, pe_fcg[fcg - 2])
                        else:
                            q.wait_ge(sPE.ap, sPE.n)
                        sWB[fcg % 2].inc(q.dma_start(
                            out=wbuf[:, 0:2048].rearrange(
                                "p (c n) -> p c n", c=NCH),
                            in_=dram["l1_w1"]
                                .rearrange("(c p) n -> p c n", p=128)
                                [:, :, hf * 4096 + fcg * 256:
                                 hf * 4096 + (fcg + 1) * 256]))
                    ld_w = (sWB[fcg % 2], sWB[fcg % 2].n)

                    for f2 in range(2):
                        fc = fcg * 2 + f2
                        bank = pb[2 + (fc % 2)]
                        h3buf = h3[0] if fc < 16 else h3[1]
                        hcol = (fc % 16) * T1

                        @section("tensor")
                        def _(t, fc=fc, f2=f2, bank=bank, wbuf=wbuf,
                              ld_w=ld_w, dve_ln=dve_ln, ab=act_base):
                            t.wait_ge(ld_w[0].ap, ld_w[1])
                            if fc == 0:
                                t.wait_ge(sDVE.ap, dve_ln)
                            if fc >= 2:
                                t.wait_ge(sACT.ap, ab + fc - 1)
                            for c in range(NCH):
                                i = t.matmul(
                                    bank[:, 0:T1],
                                    wbuf[:, c * 256 + f2 * 128:
                                         c * 256 + f2 * 128 + 128],
                                    xln[:, c * T1:(c + 1) * T1],
                                    start=(c == 0), stop=(c == NCH - 1))
                            sPE.inc(i)
                        pe_h = sPE.n
                        pe_fcg[fcg] = pe_h

                        @section("scalar")
                        def _(a, bank=bank, pe_h=pe_h, h3buf=h3buf,
                              hcol=hcol):
                            a.wait_ge(sPE.ap, pe_h)
                            sACT.inc(a.activation(
                                h3buf[:, hcol:hcol + T1], bank[:, 0:T1],
                                AF.Gelu_apprx_tanh))
                act_h = sACT.n
                pe_yod = {}
                for oc in range(NCH):
                    for half2 in range(2):
                        wbuf = G5 if half2 == 0 else G6
                        ykey = oc * 2 + half2

                        @section("sync")
                        def _(q, hf=hf, oc=oc, half2=half2, wbuf=wbuf,
                              ykey=ykey):
                            if ykey >= 2:
                                q.wait_ge(sPE.ap, pe_yod[ykey - 2])
                            else:
                                q.wait_ge(sPE.ap, pe_fcg[14 + half2])
                            sWB[half2].inc(q.dma_start(
                                out=wbuf[:, 0:2048].rearrange(
                                    "p (f n) -> p f n", f=16),
                                in_=dram["l1_w2"]
                                    .rearrange("(f p) n -> p f n", p=128)
                                    [:, hf * 32 + half2 * 16:
                                     hf * 32 + (half2 + 1) * 16,
                                     oc * 128:(oc + 1) * 128]))
                        ld_w2 = (sWB[half2], sWB[half2].n)

                        bank = pb[4 + (oc % 2)]

                        @section("tensor")
                        def _(t, oc=oc, half2=half2, wbuf=wbuf, bank=bank,
                              ld_w2=ld_w2, act_h=act_h):
                            t.wait_ge(ld_w2[0].ap, ld_w2[1])
                            t.wait_ge(sACT.ap, act_h)
                            if half2 == 0:
                                t.wait_ge(sDVE.ap, sDVE.n)
                            for f in range(16):
                                fc = half2 * 16 + f
                                h3buf = h3[0] if fc < 16 else h3[1]
                                i = t.matmul(
                                    bank[:, 0:T1],
                                    wbuf[:, f * 128:(f + 1) * 128],
                                    h3buf[:, (fc % 16) * T1:
                                          (fc % 16) * T1 + T1],
                                    start=(fc == 0), stop=(fc == 31))
                            sPE.inc(i)
                        pe_yod[oc * 2 + half2] = sPE.n
                    pe_y = sPE.n

                    @section("vector")
                    def _(v, hf=hf, oc=oc, pe_y=pe_y):
                        bank = pb[4 + (oc % 2)]
                        v.wait_ge(sPE.ap, pe_y)
                        if hf == 0:
                            sDVE.inc(v.tensor_copy(
                                yacc[:, oc * T1:(oc + 1) * T1],
                                bank[:, 0:T1]))
                        else:
                            sDVE.inc(v.tensor_tensor(
                                yacc[:, oc * T1:(oc + 1) * T1],
                                yacc[:, oc * T1:(oc + 1) * T1],
                                bank[:, 0:T1], ALU.add))
            @section("vector")
            def _(v, dve_ws=dve_ws):
                v.wait_ge(sDVE.ap, dve_ws)
                v.wait_ge(sDVE.ap, sDVE.n)
                for oc in range(NCH):
                    i = v.tensor_tensor(yacc[:, oc * T1:(oc + 1) * T1],
                                        yacc[:, oc * T1:(oc + 1) * T1],
                                        wsbc, ALU.mult)
                sDVE.inc(i)
                v.wait_ge(sDVE.ap, sDVE.n)
                for oc in range(NCH):
                    i = v.tensor_tensor(
                        yacc[:, oc * T1:(oc + 1) * T1],
                        yacc[:, oc * T1:(oc + 1) * T1],
                        mx[:, oc * T1:(oc + 1) * T1], ALU.add)
                sDVE.inc(i)
            dve_moe = sDVE.n

            @section("sync")
            def _(q):
                q.wait_ge(sDVE.ap, dve_moe)
                for c in range(NCH):
                    sST.inc(q.dma_start(
                        out=moe_out[c * 128:(c + 1) * 128, :],
                        in_=yacc[:, c * T1:(c + 1) * T1]))
            barrier()

        def phase_out(p):
            @section("sync")
            def _(q):
                for c in range(NCH):
                    sST.inc(q.dma_start(out=out[p, c * 128:(c + 1) * 128, :],
                                        in_=xres[:, c * L:(c + 1) * L]))
            barrier()

        npass = NPASS
        LIMIT = os.environ.get("KMK_LIMIT", "all")
        for p in range(npass):
            phase_load_x(p)
            phase_kv(p)
            if LIMIT != "kv":
                phase_attn0(p)
            if LIMIT not in ("kv", "attn0"):
                if p == 0:
                    phase_fold()
                phase_ffn0(p)
            if LIMIT not in ("kv", "attn0", "ffn0"):
                phase_attn1(p)
            if LIMIT not in ("kv", "attn0", "ffn0", "attn1"):
                if p == 0:
                    phase_moe()
            phase_out(p)

        @section("gpsimd")
        def _(g):
            g.wait_ge(sST.ap, sST.n)

    return nc



def kernel(**inputs):
    nc = build_program()
    in_maps = [_prep_core_inputs(inputs, c) for c in range(NCORE)]
    res = run_bass_kernel_spmd(nc, in_maps, core_ids=list(range(NCORE)))
    out = np.empty((B, L, D), dtype=np.float32)
    for c in range(NCORE):
        o = res.results[c]["out"]
        out[c] = o[0].T
        out[c + 8] = o[1].T
    for c in range(NCORE):
        mo = res.results[c]["moe_out"]
        out[0][c * 256:(c + 1) * 256] = mo[:, 0:256].T
        out[1][c * 256:(c + 1) * 256] = mo[:, 256:512].T
    return out




# version 23 (final)
# speedup vs baseline: 1.0064x; 1.0064x over previous
import os
import numpy as np
from contextlib import ExitStack

import concourse.bass as bass
import concourse.mybir as mybir
import bass_rust as _br
from concourse import library_config
from concourse.bass_utils import run_bass_kernel_spmd

F32 = mybir.dt.float32
BF16 = mybir.dt.bfloat16
AF = mybir.ActivationFunctionType
ALU = mybir.AluOpType

NCORE = 8
B, L, D, H, DH = 16, 2048, 1024, 16, 64
DF = 4096
DM = 8192
NCH = D // 128
T1 = 512
NT1 = L // T1
LNEPS = 1e-5

DEBUG = int(os.environ.get("KMK_DEBUG", "0"))
PHASES = os.environ.get("KMK_PHASES", "all")
NPASS = int(os.environ.get("KMK_NPASS", "2"))



def _pp(vec, nch):
    return np.ascontiguousarray(vec.reshape(nch, 128).T)


def _pair_pos(stage):
    s = 2 ** stage
    k = np.arange(DF // 2)
    pa = (k // s) * 2 * s + (k % s)
    pb = pa + s
    return pa, pb


def _stage_mats(weights, stage):
    pa, pb = _pair_pos(stage)
    A = np.zeros((DF // 128, 128, 128), dtype=np.float32)
    blk = pa // 128
    pal, pbl = pa % 128, pb % 128
    w = weights
    A[blk, pal, pal] = w[:, 0, 0]
    A[blk, pbl, pal] = w[:, 0, 1]
    A[blk, pal, pbl] = w[:, 1, 0]
    A[blk, pbl, pbl] = w[:, 1, 1]
    AT = np.ascontiguousarray(np.transpose(A, (0, 2, 1)))
    return A, AT


def _cross_coeffs(weights, stage, kind):
    pa, pb = _pair_pos(stage)
    self_c = np.zeros(DF, dtype=np.float32)
    part_c = np.zeros(DF, dtype=np.float32)
    w = weights
    self_c[pa] = w[:, 0, 0]
    self_c[pb] = w[:, 1, 1]
    if kind == "right":
        part_c[pa] = w[:, 0, 1]
        part_c[pb] = w[:, 1, 0]
    else:
        part_c[pa] = w[:, 1, 0]
        part_c[pb] = w[:, 0, 1]
    return self_c, part_c


def _prep_core_inputs(inputs, core):
    ip = {k: np.asarray(v) for k, v in inputs.items()}
    g = lambda k: ip[k]
    m = {}
    b0, b1 = core, core + 8
    x = g("L0_attn_x")
    m["x_fm"] = np.ascontiguousarray(
        np.stack([x[b0].T, x[b1].T]))

    for nm in ["Wq", "Wk", "Wv", "Wo"]:
        m["l0_" + nm] = np.ascontiguousarray(g(f"L0_attn_{nm}"))
    m["l0_bq"] = _pp(g("L0_attn_bq"), NCH)
    m["l0_bo"] = _pp(g("L0_attn_bo"), NCH)
    m["l0_bk_row"] = np.ascontiguousarray(g("L0_attn_bk")[None, :])
    m["l0_bv_row"] = np.ascontiguousarray(g("L0_attn_bv")[None, :])

    m["l0_WexpT"] = np.ascontiguousarray(g("L0_ffn_W_exp").T)
    m["l0_bexp_pp"] = _pp(g("L0_ffn_b_exp"), 32)
    m["l0_Wcon"] = np.ascontiguousarray(g("L0_ffn_W_con"))
    m["l0_up_bias"] = _pp(g("L0_ffn_up_bias"), DF // 128)
    m["l0_bcon"] = _pp(g("L0_ffn_b_con"), NCH)
    m["l0_dbias"] = _pp(g("L0_ffn_down_bias"), NCH)

    up_w, dn_w = g("L0_ffn_up_weights"), g("L0_ffn_down_weights")
    up_mats = []
    for s in range(7):
        A, AT = _stage_mats(up_w[s], s)
        up_mats.append(A if s == 6 else AT)
    seq = [up_mats[6], up_mats[5], up_mats[4], up_mats[3],
           up_mats[2], up_mats[1], up_mats[0]]
    m["bfu_mats"] = np.ascontiguousarray(
        np.stack(seq, axis=1))
    dn_mats = []
    for s in range(7):
        A, AT = _stage_mats(dn_w[s], s)
        dn_mats.append(AT if s == 0 else A)
    seq = [dn_mats[0], dn_mats[1], dn_mats[2], dn_mats[3],
           dn_mats[4], dn_mats[5], dn_mats[6]]
    m["bfd_mats"] = np.ascontiguousarray(np.stack(seq, axis=1))

    ucs, ucp, dcs, dcp = [], [], [], []
    for s in range(7, 12):
        sc, pc = _cross_coeffs(up_w[s], s, "right")
        ucs.append(_pp(sc, 32))
        ucp.append(_pp(pc, 32))
        sc, pc = _cross_coeffs(dn_w[s], s, "left")
        dcs.append(_pp(sc, 32))
        dcp.append(_pp(pc, 32))
    m["bfu_self"] = np.ascontiguousarray(np.concatenate(ucs, axis=1))
    m["bfu_part"] = np.ascontiguousarray(np.concatenate(ucp, axis=1))
    m["bfd_self"] = np.ascontiguousarray(np.concatenate(dcs, axis=1))
    m["bfd_part"] = np.ascontiguousarray(np.concatenate(dcp, axis=1))

    for nm in ["Wq", "Wk", "Wv"]:
        w = g(f"L1_attn_{nm}")
        m["l1_" + nm] = np.ascontiguousarray(
            np.transpose(w, (1, 0, 2)).reshape(D, H * DH))
    m["l1_projT"] = np.ascontiguousarray(g("L1_attn_proj_w").T)
    m["l1_projb"] = _pp(g("L1_attn_proj_b"), NCH)

    m["l1_w1"] = np.ascontiguousarray(g("L1_ffn_w1"))
    m["l1_w2"] = np.ascontiguousarray(g("L1_ffn_w2"))
    ew = g("L1_ffn_expert_weights")
    sl0 = ew[core * 256:(core + 1) * 256]
    sl1 = ew[2048 + core * 256:2048 + (core + 1) * 256]
    m["l1_ws2"] = np.ascontiguousarray(
        np.concatenate([sl0, sl1], axis=0).T)

    cv = np.zeros((128, 8), dtype=np.float32)
    cv[:, 0] = 1e-5
    cv[:, 1] = 1.0 / D
    cv[:, 2] = -1.0 / D
    cv[:, 3] = 1.0 / float(L * DH)
    cv[:, 4] = 0.125
    cv[:, 5] = -1e30
    cv[:, 6] = -1.0
    m["cvals"] = cv
    m["identity"] = np.eye(128, dtype=np.float32)
    m["ones_col"] = np.ones((128, 1), dtype=np.float32)
    m["ones2"] = np.ones((2, 128), dtype=np.float32)
    p = np.arange(128)[:, None]
    mcol = np.arange(2080)[None, :]
    m["l1_maskbig"] = np.ascontiguousarray(
        np.where(p * 16 > mcol - 15, np.float32(-1e30), np.float32(0.0))
        .astype(np.float32))
    return m



class Sem:
    def __init__(self, es, nc, name, step=1):
        self.ap = es.enter_context(nc.semaphore(name))
        self.step = step
        self.n = 0

    def inc(self, instr, k=1):
        instr.then_inc(self.ap, self.step * k)
        self.n += self.step * k
        return self.n


def build_program():
    nc = bass.Bass("TRN2", target_bir_lowering=False)

    x_fm = nc.dram_tensor("x_fm", [2, D, L], F32, kind="ExternalInput").ap()
    out = nc.dram_tensor("out", [2, D, L], F32, kind="ExternalOutput").ap()
    dram = {}
    for nm, shp in [
        ("l0_Wq", [D, D]), ("l0_Wk", [D, D]), ("l0_Wv", [D, D]), ("l0_Wo", [D, D]),
        ("l0_bq", [128, NCH]), ("l0_bo", [128, NCH]),
        ("l0_bk_row", [1, D]), ("l0_bv_row", [1, D]),
        ("l0_WexpT", [DF, D]), ("l0_bexp_pp", [128, 32]), ("l0_Wcon", [DF, D]),
        ("l0_up_bias", [128, 32]), ("l0_bcon", [128, NCH]), ("l0_dbias", [128, NCH]),
        ("bfu_mats", [32, 7, 128, 128]), ("bfd_mats", [32, 7, 128, 128]),
        ("bfu_self", [128, 5 * 32]), ("bfu_part", [128, 5 * 32]),
        ("bfd_self", [128, 5 * 32]), ("bfd_part", [128, 5 * 32]),
        ("l1_Wq", [D, D]), ("l1_Wk", [D, D]), ("l1_Wv", [D, D]),
        ("l1_projT", [D, D]), ("l1_projb", [128, NCH]),
        ("l1_w1", [D, DM]), ("l1_w2", [DM, D]),
        ("l1_ws2", [2, 512]),
        ("identity", [128, 128]), ("ones_col", [128, 1]),
        ("ones2", [2, 128]),
        ("l1_maskbig", [128, 2080]), ("cvals", [128, 8]),
    ]:
        dram[nm] = nc.dram_tensor(nm, shp, F32, kind="ExternalInput").ap()
    wexp_f = nc.dram_tensor("wexp_fold", [D, DF], F32, kind="Internal").ap()
    wexp_h = nc.dram_tensor("wexp_hi", [D, DF], BF16, kind="Internal").ap()
    wexp_l = nc.dram_tensor("wexp_lo", [D, DF], BF16, kind="Internal").ap()
    a2a_in = nc.dram_tensor("a2a_in", [8, D, 256], F32, kind="Internal")
    a2a_out = nc.dram_tensor("a2a_out", [8, D, 256], F32, kind="Internal")
    moe_out = nc.dram_tensor("moe_out", [D, 512], F32, kind="ExternalOutput").ap()
    wcon_f = nc.dram_tensor("wcon_fold", [DF, D], F32, kind="Internal").ap()
    dbg = {}
    if DEBUG:
        for nm, shp in [("dbg_x1", [D, L]), ("dbg_x2", [D, L]), ("dbg_x3", [D, L]),
                        ("dbg_kv", [64, H * DH]), ("dbg_xln", [D, T1]),
                        ("dbg_wexpf", [D, DF]), ("dbg_wconf", [DF, D]),
                        ("dbg_bexp", [128, 32])]:
            dbg[nm] = nc.dram_tensor(nm, shp, F32, kind="ExternalOutput").ap()

    es = ExitStack()
    with es:
        sb = lambda name, shp: es.enter_context(nc.sbuf_tensor(name, shp, F32)).ap()
        xres = sb("xres", [128, NCH * L])
        xln = sb("xln", [128, NCH * T1])
        ident = sb("ident", [128, 128])
        bq = sb("bq", [128, NCH]); bo = sb("bo", [128, NCH])
        projb = sb("projb", [128, NCH])
        bcd = sb("bcd", [128, 2 * NCH])
        upb = sb("upb", [128, 32])
        btot = sb("btot", [128, 32])
        ones = sb("ones", [128, 1])
        ones2 = sb("ones2t", [2, 128])
        onesr = ones2[0:1, :]
        cval = sb("cval", [128, 8])
        for k, v in [(0, 1e-5), (1, 1.0 / D), (2, -1.0 / D),
                     (3, 1.0 / float(L * DH)), (4, 0.125), (5, -1e30),
                     (6, -1.0)]:
            nc.const_aps.register(F32, v, cval[:, k:k + 1])
        st_m = sb("st_m", [1, T1])
        st_q = sb("st_q", [1, T1])
        st_nm = sb("st_nm", [1, T1])
        nmb = sb("nmb", [128, T1]); rsb = sb("rsb", [128, T1])
        G1 = sb("G1", [128, 8192])
        G2 = sb("G2", [128, 8192])
        G3 = sb("G3", [128, 4096])
        G4 = sb("G4", [128, 5120])
        G5 = sb("G5", [128, 2048])
        G6 = sb("G6", [128, 2080])
        kvacc = G4[0:64, 4096:5120]
        coef = G4[:, 1280:1920]

        pb = [es.enter_context(nc.psum_tensor(f"pb{i}", [128, 512], F32)).ap()
              for i in range(8)]

        sems = {}
        for nm, step in [("ld", 16), ("st", 16), ("pe", 1), ("dve", 1),
                         ("act", 1), ("gp", 1), ("oo", 16),
                         ("wb0", 16), ("wb1", 16),
                         ("wc0", 16), ("wc1", 16), ("wc2", 16), ("wc3", 16)]:
            sems[nm] = Sem(es, nc, "s_" + nm, step)
        sLD, sST, sPE, sDVE, sACT, sGP, sOO = (sems[k] for k in
                                               ["ld", "st", "pe", "dve",
                                                "act", "gp", "oo"])
        sWB = [sems["wb0"], sems["wb1"]]
        sWC = [sems["wc0"], sems["wc1"], sems["wc2"], sems["wc3"]]

        blk = es.enter_context(nc.Block())

        def section(engname):
            dec = getattr(blk, engname)
            def wrap(fn):
                dec(fn)
            return wrap

        def barrier():
            snap = [(s, s.n) for s in sems.values() if s.n > 0]
            for engname in ["sync", "tensor", "vector", "scalar", "gpsimd"]:
                @section(engname)
                def _(q, snap=snap):
                    for s, v in snap:
                        q.wait_ge(s.ap, v)

        @section("sync")
        def _(q):
            for nm, dst in [("identity", ident), ("cvals", cval),
                            ("l0_bq", bq), ("l0_bo", bo),
                            ("l1_projb", projb), ("l0_up_bias", upb),
                            ("ones_col", ones), ("ones2", ones2)]:
                sLD.inc(q.dma_start(out=dst, in_=dram[nm]))
            sLD.inc(q.dma_start(out=bcd[:, 0:NCH], in_=dram["l0_bcon"]))
            sLD.inc(q.dma_start(out=bcd[:, NCH:2 * NCH], in_=dram["l0_dbias"]))
        ld_const = sLD.n

        @section("vector")
        def _(v):
            v.wait_ge(sLD.ap, ld_const)
            sDVE.inc(v.memset(kvacc, 0.0))
            sDVE.inc(v.tensor_tensor(bcd[:, 0:NCH], bcd[:, 0:NCH],
                                     bcd[:, NCH:2 * NCH], ALU.add))

        barrier()

        def emit_ln_stats(t0, T, src=None, sstr=None, ld_wait=None,
                          dve_free=None):
            if src is None:
                src, sstr = xres, L
            dve_prior = sDVE.n if dve_free is None else dve_free

            @section("tensor")
            def _(t):
                if ld_wait is not None:
                    t.wait_ge(sLD.ap, ld_wait)
                t.wait_ge(sDVE.ap, dve_prior)
                for c in range(NCH):
                    i = t.matmul(pb[0][0:1, 0:T], ones,
                                 src[:, c * sstr + t0:c * sstr + t0 + T],
                                 start=(c == 0), stop=(c == NCH - 1))
                sPE.inc(i)
            pe_xs = sPE.n

            pe_sq = {}
            for c in range(NCH):
                scr = nmb if c % 2 == 0 else rsb

                @section("scalar")
                def _(a, c=c, scr=scr):
                    if c == 0 and ld_wait is not None:
                        a.wait_ge(sLD.ap, ld_wait)
                    if c < 2:
                        a.wait_ge(sDVE.ap, dve_prior)
                    else:
                        a.wait_ge(sPE.ap, pe_sq[c - 2])
                    sACT.inc(a.activation(
                        scr[:, 0:T], src[:, c * sstr + t0:c * sstr + t0 + T],
                        AF.Square))
                act_c = sACT.n

                @section("tensor")
                def _(t, c=c, scr=scr, act_c=act_c):
                    t.wait_ge(sACT.ap, act_c)
                    sPE.inc(t.matmul(pb[1][0:1, 0:T], ones, scr[:, 0:T],
                                     start=(c == 0), stop=(c == NCH - 1)))
                pe_sq[c] = sPE.n
            pe_sum = sPE.n

            @section("vector")
            def _(v):
                v.wait_ge(sPE.ap, pe_sum)
                v.tensor_scalar(st_m[:, 0:T], pb[0][0:1, 0:T], 1.0 / D, None,
                                ALU.mult)
                v.tensor_scalar(st_q[:, 0:T], pb[1][0:1, 0:T], 1.0 / D, None,
                                ALU.mult)
                sDVE.inc(v.tensor_scalar(st_nm[:, 0:T], pb[0][0:1, 0:T],
                                         -1.0 / D, None, ALU.mult))
                v.wait_ge(sDVE.ap, sDVE.n)
                sDVE.inc(v.tensor_tensor(st_m[:, 0:T], st_m[:, 0:T], st_m[:, 0:T],
                                         ALU.mult))
                v.wait_ge(sDVE.ap, sDVE.n)
                sDVE.inc(v.tensor_tensor(st_q[:, 0:T], st_q[:, 0:T],
                                         st_m[:, 0:T], ALU.subtract))
            dve_var = sDVE.n

            @section("scalar")
            def _(a):
                a.wait_ge(sDVE.ap, dve_var)
                sACT.inc(a.activation(st_q[:, 0:T], st_q[:, 0:T], AF.Sqrt,
                                      bias=LNEPS))
            act_sd = sACT.n

            @section("vector")
            def _(v):
                v.wait_ge(sACT.ap, act_sd)
                sDVE.inc(v.reciprocal(st_m[:, 0:T], st_q[:, 0:T]))
            dve_rs = sDVE.n

            @section("tensor")
            def _(t):
                t.wait_ge(sDVE.ap, dve_rs)
                sPE.inc(t.matmul(pb[0][:, 0:T], onesr, st_nm[:, 0:T],
                                 start=True, stop=True))
                sPE.inc(t.matmul(pb[1][:, 0:T], onesr, st_m[:, 0:T],
                                 start=True, stop=True))
            pe_bc = sPE.n

            @section("vector")
            def _(v):
                v.wait_ge(sPE.ap, pe_bc)
                v.tensor_copy(nmb[:, 0:T], pb[0][:, 0:T])
                sDVE.inc(v.tensor_copy(rsb[:, 0:T], pb[1][:, 0:T]))
            return sDVE.n

        def emit_ln_apply(t0, T, dve_stats, src=None, sstr=None):
            if src is None:
                src, sstr = xres, L
            pe_prior = sPE.n

            @section("vector")
            def _(v):
                v.wait_ge(sDVE.ap, dve_stats)
                if pe_prior:
                    v.wait_ge(sPE.ap, pe_prior)
                for c in range(NCH):
                    i = v.tensor_tensor(xln[:, c * T1:c * T1 + T],
                                        src[:, c * sstr + t0:c * sstr + t0 + T],
                                        nmb[:, 0:T], ALU.add)
                sDVE.inc(i)
                v.wait_ge(sDVE.ap, sDVE.n)
                for c in range(NCH):
                    i = v.tensor_tensor(xln[:, c * T1:c * T1 + T],
                                        xln[:, c * T1:c * T1 + T],
                                        rsb[:, 0:T], ALU.mult)
                sDVE.inc(i)

        def emit_ln(t0, T, tap=None, src=None, sstr=None, ld_wait=None):
            ds = emit_ln_stats(t0, T, src=src, sstr=sstr, ld_wait=ld_wait)
            emit_ln_apply(t0, T, ds, src=src, sstr=sstr)

        def phase_load_x(p):
            @section("sync")
            def _(q):
                for c in range(NCH):
                    i = q.dma_start(out=xres[:, c * L:(c + 1) * L],
                                    in_=x_fm[p, c * 128:(c + 1) * 128, :])
                    sLD.inc(i)
            barrier()

        def phase_kv(p):
            Wk, Wv = G1, G2
            ktok = G3
            vtok = G4[:, 0:4096]
            bkb = G5[:, 0:1024]
            bvb = G6[:, 0:1024]

            @section("sync")
            def _(q):
                sLD.inc(q.dma_start(out=Wk.rearrange("p (c n) -> p c n", c=NCH),
                                    in_=dram["l0_Wk"].rearrange("(c p) n -> p c n", p=128)))
                sLD.inc(q.dma_start(out=Wv.rearrange("p (c n) -> p c n", c=NCH),
                                    in_=dram["l0_Wv"].rearrange("(c p) n -> p c n", p=128)))
                sLD.inc(q.dma_start(out=G5[0:1, 1024:2048], in_=dram["l0_bk_row"]))
                sLD.inc(q.dma_start(out=G6[0:1, 1024:2048], in_=dram["l0_bv_row"]))
            ld_w = sLD.n

            for half in range(2):
                @section("tensor")
                def _(t, half=half):
                    t.wait_ge(sLD.ap, ld_w)
                    if half == 1:
                        t.wait_ge(sDVE.ap, sDVE.n)
                    sPE.inc(t.matmul(
                        pb[2][:, 0:512], onesr,
                        G5[0:1, 1024 + half * 512:1024 + half * 512 + 512],
                        start=True, stop=True))
                    sPE.inc(t.matmul(
                        pb[3][:, 0:512], onesr,
                        G6[0:1, 1024 + half * 512:1024 + half * 512 + 512],
                        start=True, stop=True))
                pe_bias = sPE.n

                @section("vector")
                def _(v, half=half, pe_bias=pe_bias):
                    v.wait_ge(sPE.ap, pe_bias)
                    v.tensor_copy(bkb[:, half * 512:half * 512 + 512],
                                  pb[2][:, 0:512])
                    sDVE.inc(v.tensor_copy(bvb[:, half * 512:half * 512 + 512],
                                           pb[3][:, 0:512]))

            @section("vector")
            def _(v):
                sDVE.inc(v.memset(kvacc, 0.0))

            ds_ln = emit_ln_stats(0, T1)
            for ti in range(NT1):
                t0 = ti * T1
                emit_ln_apply(t0, T1, ds_ln)
                dve_ln = sDVE.n
                for tc in range(T1 // 128):
                    for half in range(2):
                        bankk = pb[2 + half]
                        bankv = pb[4 + half]

                        @section("tensor")
                        def _(t, tc=tc, half=half, bankk=bankk, bankv=bankv,
                              dve_ln=dve_ln):
                            t.wait_ge(sLD.ap, ld_w)
                            t.wait_ge(sDVE.ap, dve_ln)
                            t.wait_ge(sDVE.ap, sDVE.n)
                            for c in range(NCH):
                                i = t.matmul(
                                    bankk[:, 0:512],
                                    xln[:, c * T1 + tc * 128:c * T1 + tc * 128 + 128],
                                    Wk[:, c * D + half * 512:c * D + half * 512 + 512],
                                    start=(c == 0), stop=(c == NCH - 1))
                            sPE.inc(i)
                            for c in range(NCH):
                                i = t.matmul(
                                    bankv[:, 0:512],
                                    xln[:, c * T1 + tc * 128:c * T1 + tc * 128 + 128],
                                    Wv[:, c * D + half * 512:c * D + half * 512 + 512],
                                    start=(c == 0), stop=(c == NCH - 1))
                            sPE.inc(i)
                        pe_kv = sPE.n

                        @section("vector")
                        def _(v, tc=tc, half=half, bankk=bankk, bankv=bankv,
                              pe_kv=pe_kv):
                            v.wait_ge(sPE.ap, pe_kv)
                            v.tensor_tensor(
                                ktok[:, tc * D + half * 512:tc * D + half * 512 + 512],
                                bankk[:, 0:512],
                                G5[:, half * 512:half * 512 + 512], ALU.add)
                            sDVE.inc(v.tensor_tensor(
                                vtok[:, tc * D + half * 512:tc * D + half * 512 + 512],
                                bankv[:, 0:512],
                                G6[:, half * 512:half * 512 + 512], ALU.add))
                        dve_ev = sDVE.n

                        @section("scalar")
                        def _(a, tc=tc, half=half, dve_ev=dve_ev):
                            a.wait_ge(sDVE.ap, dve_ev)
                            sACT.inc(a.activation(
                                ktok[:, tc * D + half * 512:tc * D + half * 512 + 512],
                                ktok[:, tc * D + half * 512:tc * D + half * 512 + 512],
                                AF.Relu))
                    act_relu = sACT.n
                    dve_base = sDVE.n
                    for h in range(H):
                        bank = pb[6 + (h % 2)]

                        @section("tensor")
                        def _(t, tc=tc, h=h, bank=bank, act_relu=act_relu,
                              dve_base=dve_base):
                            t.wait_ge(sACT.ap, act_relu)
                            if h >= 2:
                                t.wait_ge(sDVE.ap, dve_base + (h - 1))
                            i = t.matmul(bank[0:64, 0:64],
                                         ktok[:, tc * D + h * 64:tc * D + h * 64 + 64],
                                         vtok[:, tc * D + h * 64:tc * D + h * 64 + 64],
                                         start=True, stop=True)
                            sPE.inc(i)
                        pe_h = sPE.n

                        @section("vector")
                        def _(v, h=h, bank=bank, pe_h=pe_h):
                            v.wait_ge(sPE.ap, pe_h)
                            sDVE.inc(v.tensor_tensor(
                                G4[0:64, 4096 + h * 64:4096 + h * 64 + 64],
                                G4[0:64, 4096 + h * 64:4096 + h * 64 + 64],
                                bank[0:64, 0:64], ALU.add))
                if ti + 1 < NT1:
                    ds_ln = emit_ln_stats((ti + 1) * T1, T1, dve_free=dve_ln)
            dve_all = sDVE.n

            @section("scalar")
            def _(a):
                a.wait_ge(sDVE.ap, dve_all)
                sACT.inc(a.activation(kvacc, kvacc, AF.Copy,
                                      scale=1.0 / float(L * DH)))
            act_kvs = sACT.n

            @section("sync")
            def _(q):
                q.wait_ge(sACT.ap, act_kvs)
                sLD.inc(q.dma_start(out=G4[64:128, 4096:5120],
                                    in_=G4[0:64, 4096:5120]))
            if DEBUG:
                act_kv = sACT.n

                @section("sync")
                def _(q):
                    q.wait_ge(sACT.ap, act_kv)
                    sST.inc(q.dma_start(out=dbg["dbg_kv"], in_=kvacc))
            barrier()

        def phase_attn0(p):
            Wq, Wo = G1, G2
            qT = G3
            attnT = G4[:, 0:4096]
            ds_ln = None

            @section("sync")
            def _(q):
                sLD.inc(q.dma_start(out=Wq.rearrange("p (c n) -> p c n", c=NCH),
                                    in_=dram["l0_Wq"].rearrange("(c p) n -> p c n", p=128)))
                sLD.inc(q.dma_start(out=Wo.rearrange("p (c n) -> p c n", c=NCH),
                                    in_=dram["l0_Wo"].rearrange("(c p) n -> p c n", p=128)))
            ld_w = sLD.n
            ds_ln = emit_ln_stats(0, T1)
            for ti in range(NT1):
                t0 = ti * T1
                emit_ln_apply(t0, T1, ds_ln)
                dve_ln = sDVE.n
                for hp in range(8):
                    bank = pb[2 + (hp % 2)]

                    @section("tensor")
                    def _(t, hp=hp, bank=bank, dve_ln=dve_ln):
                        t.wait_ge(sLD.ap, ld_w)
                        t.wait_ge(sDVE.ap, dve_ln)
                        if hp >= 2:
                            t.wait_ge(sACT.ap, sACT.n)
                        for c in range(NCH):
                            i = t.matmul(
                                bank[:, 0:T1],
                                Wq[:, c * D + hp * 128:c * D + hp * 128 + 128],
                                xln[:, c * T1:(c + 1) * T1],
                                start=(c == 0), stop=(c == NCH - 1))
                        sPE.inc(i)
                    pe_q = sPE.n

                    @section("scalar")
                    def _(a, hp=hp, bank=bank, pe_q=pe_q):
                        a.wait_ge(sPE.ap, pe_q)
                        sACT.inc(a.activation(qT[:, hp * T1:(hp + 1) * T1],
                                              bank[:, 0:T1], AF.Relu,
                                              bias=bq[:, hp:hp + 1]))
                act_q = sACT.n
                for h in range(H):
                    bank = pb[4 + (h % 2)]

                    @section("tensor")
                    def _(t, h=h, bank=bank, act_q=act_q):
                        t.wait_ge(sACT.ap, act_q + (h - 1 if h >= 2 else 0))
                        base = (h % 2) * 64
                        i = t.matmul(bank[base:base + 64, 0:T1],
                                     G4[base:base + 64,
                                        4096 + h * 64:4096 + h * 64 + 64],
                                     qT[base:base + 64,
                                        (h // 2) * T1:(h // 2 + 1) * T1],
                                     start=True, stop=True,
                                     tile_position=(base, base))
                        sPE.inc(i)
                    pe_a = sPE.n

                    @section("scalar")
                    def _(a, h=h, bank=bank, pe_a=pe_a):
                        a.wait_ge(sPE.ap, pe_a)
                        base = (h % 2) * 64
                        sACT.inc(a.activation(
                            attnT[base:base + 64,
                                  (h // 2) * T1:(h // 2 + 1) * T1],
                            bank[base:base + 64, 0:T1], AF.Copy))
                act_attn = sACT.n
                dve_wo = {}
                for oc in range(NCH):
                    bank = pb[6 + (oc % 2)]

                    @section("tensor")
                    def _(t, oc=oc, bank=bank, act_attn=act_attn):
                        if oc == 0:
                            t.wait_ge(sACT.ap, act_attn)
                        if oc >= 2:
                            t.wait_ge(sDVE.ap, dve_wo[oc - 2])
                        for c in range(NCH):
                            i = t.matmul(
                                bank[:, 0:T1],
                                Wo[:, c * D + oc * 128:c * D + oc * 128 + 128],
                                attnT[:, c * T1:(c + 1) * T1],
                                start=(c == 0), stop=(c == NCH - 1))
                        sPE.inc(i)
                    pe_o = sPE.n

                    @section("vector")
                    def _(v, oc=oc, bank=bank, pe_o=pe_o, t0=t0):
                        v.wait_ge(sPE.ap, pe_o)
                        sDVE.inc(v.scalar_tensor_tensor(
                            xres[:, oc * L + t0:oc * L + t0 + T1],
                            bank[:, 0:T1], bo[:, oc:oc + 1],
                            xres[:, oc * L + t0:oc * L + t0 + T1],
                            ALU.add, ALU.add))
                    dve_wo[oc] = sDVE.n
                if ti + 1 < NT1:
                    ds_ln = emit_ln_stats((ti + 1) * T1, T1, dve_free=dve_ln)
            if DEBUG:
                dve_x1 = sDVE.n

                @section("sync")
                def _(q):
                    q.wait_ge(sDVE.ap, dve_x1)
                    for c in range(NCH):
                        sST.inc(q.dma_start(
                            out=dbg["dbg_x1"][c * 128:(c + 1) * 128, :],
                            in_=xres[:, c * L:(c + 1) * L]))
            barrier()

        def _cross_stage_dve(bufA, bufB, si, arr_s, arr_p, stage, slabw,
                             pe_wait=None, ld_wait=None):
            d = 2 ** (stage - 7)

            @section("vector")
            def _(v):
                v.wait_ge(sDVE.ap, sDVE.n)
                if pe_wait:
                    v.wait_ge(sPE.ap, pe_wait)
                if ld_wait:
                    v.wait_ge(sLD.ap, ld_wait)
                for ch in range(32):
                    i = v.tensor_scalar(
                        bufB[:, ch * slabw:(ch + 1) * slabw],
                        bufA[:, (ch ^ d) * slabw:((ch ^ d) + 1) * slabw],
                        coef[:, arr_p * 160 + si * 32 + ch:
                             arr_p * 160 + si * 32 + ch + 1],
                        None, ALU.mult)
                sDVE.inc(i)
                v.wait_ge(sDVE.ap, sDVE.n)
                for ch in range(32):
                    i = v.scalar_tensor_tensor(
                        bufB[:, ch * slabw:(ch + 1) * slabw],
                        bufA[:, ch * slabw:(ch + 1) * slabw],
                        coef[:, arr_s * 160 + si * 32 + ch:
                             arr_s * 160 + si * 32 + ch + 1],
                        bufB[:, ch * slabw:(ch + 1) * slabw],
                        ALU.mult, ALU.add)
                sDVE.inc(i)

        def phase_fold():
            coef_base = 1280
            smats = [G4[:, 0:896], G4[:, 1920:2816]]
            vpp = [G4[:, 896:1024], G4[:, 1024:1152]]
            ttmps = [G4[:, 1152:1280], G4[:, 2816:2944],
                     G4[:, 2944:3072], G4[:, 3072:3200]]
            devac = [G4[:, 3200:3456], G4[:, 3456:3712]]
            Acmp = G3

            @section("sync")
            def _(q):
                for i, nm in enumerate(["bfu_self", "bfu_part",
                                        "bfd_self", "bfd_part"]):
                    sLD.inc(q.dma_start(
                        out=G4[:, coef_base + i * 160:coef_base + (i + 1) * 160],
                        in_=dram[nm]))
            ld_coef = sLD.n

            for updown in range(2):
                mats_nm = "bfu_mats" if updown == 0 else "bfd_mats"
                pe_blk = {}
                for bp in range(16):
                    pair = (2 * bp, 2 * bp + 1)
                    ld_ms = {}
                    for w2, b in enumerate(pair):
                        smat = smats[w2]

                        @section("sync")
                        def _(q, b=b, w2=w2, smat=smat, bp=bp):
                            if bp >= 1:
                                q.wait_ge(sPE.ap, pe_blk[b - 2])
                            elif updown == 1:
                                q.wait_ge(sPE.ap, sPE.n)
                            sWB[w2].inc(q.dma_start(
                                out=smat.rearrange("p (s c) -> p s c", s=7),
                                in_=dram[mats_nm][b]
                                    .rearrange("s p c -> p s c")))
                        ld_ms[w2] = (sWB[w2], sWB[w2].n)

                    dve_step = {}
                    @section("vector")
                    def _(v, ld_ms=ld_ms, bp=bp):
                        for w2 in range(2):
                            v.wait_ge(ld_ms[w2][0].ap, ld_ms[w2][1])
                            if bp >= 1:
                                v.wait_ge(sPE.ap, pe_blk[2 * bp - 2 + w2])
                            sDVE.inc(v.tensor_copy(vpp[w2],
                                                   smats[w2][:, 0:128]))
                    dve_step[(0, 0)] = sDVE.n - 1
                    dve_step[(1, 0)] = sDVE.n

                    pe_step = {}
                    for j in range(1, 7):
                        for w2, b in enumerate(pair):
                            bank = pb[w2 * 2 + (j % 2)]

                            @section("tensor")
                            def _(t, j=j, w2=w2, bank=bank, ld_ms=ld_ms):
                                t.wait_ge(ld_ms[w2][0].ap, ld_ms[w2][1])
                                t.wait_ge(sDVE.ap, dve_step[(w2, j - 1)])
                                sPE.inc(t.matmul(
                                    bank[:, 0:128],
                                    smats[w2][:, j * 128:(j + 1) * 128],
                                    vpp[w2], start=True, stop=True))
                            pe_step[(w2, j)] = sPE.n
                        for w2, b in enumerate(pair):
                            bank = pb[w2 * 2 + (j % 2)]

                            @section("vector")
                            def _(v, j=j, w2=w2, b=b, bank=bank):
                                v.wait_ge(sPE.ap, pe_step[(w2, j)])
                                if j < 6:
                                    sDVE.inc(v.tensor_copy(vpp[w2],
                                                           bank[:, 0:128]))
                                else:
                                    sDVE.inc(v.tensor_copy(
                                        Acmp[:, b * 128:(b + 1) * 128],
                                        bank[:, 0:128]))
                            dve_step[(w2, j)] = sDVE.n
                    for w2, b in enumerate(pair):
                        pe_blk[b] = pe_step[(w2, 6)]
                wsrc = dram["l0_WexpT"] if updown == 0 else dram["l0_Wcon"]
                wdst = wexp_f if updown == 0 else wcon_f
                nslab = 5 if updown == 0 else 4
                for slab in range(nslab):
                    slabw = 1 if (updown == 0 and slab == 4) else 256
                    c0 = slab * 256

                    sstr = 256 if slabw == 256 else 1

                    @section("sync")
                    def _(q, c0=c0, slabw=slabw):
                        q.wait_ge(sPE.ap, sPE.n)
                        q.wait_ge(sDVE.ap, sDVE.n)
                        if slabw == 256:
                            sLD.inc(q.dma_start(
                                out=G1.rearrange("p (b n) -> p b n", n=256),
                                in_=wsrc.rearrange("(b p) n -> p b n", p=128)
                                    [:, :, c0:c0 + 256]))
                        else:
                            sLD.inc(q.dma_start(out=G1[:, 0:32],
                                                in_=dram["l0_bexp_pp"]))
                    ld_s = sLD.n

                    if updown == 0:
                        for b in range(32):
                            bank = pb[b % 2]

                            @section("tensor")
                            def _(t, b=b, bank=bank, ld_s=ld_s, slabw=slabw,
                                  sstr=sstr):
                                t.wait_ge(sLD.ap, ld_s)
                                t.wait_ge(sDVE.ap, sDVE.n)
                                sPE.inc(t.matmul(
                                    bank[:, 0:slabw],
                                    Acmp[:, b * 128:(b + 1) * 128],
                                    G1[:, b * sstr:b * sstr + slabw],
                                    start=True, stop=True))
                            pe_b = sPE.n

                            @section("vector")
                            def _(v, b=b, bank=bank, pe_b=pe_b, slabw=slabw,
                                  sstr=sstr):
                                v.wait_ge(sPE.ap, pe_b)
                                sDVE.inc(v.tensor_copy(
                                    G2[:, b * sstr:b * sstr + slabw],
                                    bank[:, 0:slabw]))
                        cur, oth = G2, G1
                        pe_bf = sPE.n
                        for si in range(5):
                            _cross_stage_dve(cur, oth, si, 0, 1, 7 + si,
                                             256 if slabw == 256 else 1,
                                             pe_wait=pe_bf, ld_wait=ld_s)
                            cur, oth = oth, cur
                        final = cur
                        dve_f = sDVE.n
                        if slabw == 1:
                            @section("vector")
                            def _(v, final=final, dve_f=dve_f):
                                v.wait_ge(sDVE.ap, dve_f)
                                sDVE.inc(v.tensor_tensor(
                                    btot, final[:, 0:32], upb, ALU.add))
                        else:
                            dve_tpf = {}
                            for b in range(32):
                                for c2 in range(2):
                                    ti2 = b * 2 + c2
                                    bank = pb[2 + (ti2 % 2)]
                                    ttmp = ttmps[(ti2 % 2) * 2]
                                    tbc = ttmps[(ti2 % 2) * 2 + 1].bitcast(BF16)
                                    wc = sWC[ti2 % 2]

                                    @section("tensor")
                                    def _(t, b=b, c2=c2, bank=bank,
                                          dve_f=dve_f, final=final,
                                          ti2=ti2):
                                        if ti2 == 0:
                                            t.wait_ge(sDVE.ap, dve_f)
                                        if ti2 >= 2:
                                            t.wait_ge(sDVE.ap, dve_tpf[ti2 - 2])
                                        sPE.inc(t.transpose(
                                            bank[:, 0:128],
                                            final[:, b * 256 + c2 * 128:
                                                  b * 256 + c2 * 128 + 128],
                                            ident))
                                    pe_t = sPE.n

                                    @section("vector")
                                    def _(v, bank=bank, pe_t=pe_t, ti2=ti2,
                                          ttmp=ttmp, tbc=tbc, wc=wc):
                                        v.wait_ge(sPE.ap, pe_t)
                                        if wc.n > 0:
                                            v.wait_ge(wc.ap, wc.n)
                                        sDVE.inc(v.tensor_copy(ttmp,
                                                               bank[:, 0:128]))
                                        v.wait_ge(sDVE.ap, sDVE.n)
                                        sDVE.inc(v.tensor_copy(tbc[:, 0:128],
                                                               ttmp))
                                        v.wait_ge(sDVE.ap, sDVE.n)
                                        sDVE.inc(v.tensor_tensor(
                                            ttmp, ttmp, tbc[:, 0:128],
                                            ALU.subtract))
                                        v.wait_ge(sDVE.ap, sDVE.n)
                                        sDVE.inc(v.tensor_copy(
                                            tbc[:, 128:256], ttmp))
                                    dve_t2 = sDVE.n
                                    dve_tpf[ti2] = dve_t2

                                    @section("sync")
                                    def _(q, b=b, c2=c2, c0=c0, dve_t2=dve_t2,
                                          tbc=tbc, wc=wc):
                                        q.wait_ge(sDVE.ap, dve_t2)
                                        wc.inc(q.dma_start(
                                            out=wexp_h[c0 + c2 * 128:
                                                       c0 + c2 * 128 + 128,
                                                       b * 128:(b + 1) * 128],
                                            in_=tbc[:, 0:128]))
                                        wc.inc(q.dma_start(
                                            out=wexp_l[c0 + c2 * 128:
                                                       c0 + c2 * 128 + 128,
                                                       b * 128:(b + 1) * 128],
                                            in_=tbc[:, 128:256]))
                    else:
                        cur, oth = G1, G2
                        pe_bf = sPE.n
                        for si in [4, 3, 2, 1, 0]:
                            _cross_stage_dve(cur, oth, si, 2, 3, 7 + si, 256,
                                             pe_wait=pe_bf, ld_wait=ld_s)
                            cur, oth = oth, cur
                        dve_f = sDVE.n
                        dve_bdf = {}
                        for b in range(32):
                            bank = pb[b % 2]
                            dv = devac[b % 2]
                            wc = sWC[b % 2]

                            @section("tensor")
                            def _(t, b=b, bank=bank, dve_f=dve_f, cur=cur):
                                if b == 0:
                                    t.wait_ge(sDVE.ap, dve_f)
                                if b >= 2:
                                    t.wait_ge(sDVE.ap, dve_bdf[b - 2])
                                sPE.inc(t.matmul(
                                    bank[:, 0:256],
                                    Acmp[:, b * 128:(b + 1) * 128],
                                    cur[:, b * 256:(b + 1) * 256],
                                    start=True, stop=True))
                            pe_b = sPE.n

                            @section("vector")
                            def _(v, b=b, bank=bank, pe_b=pe_b, dv=dv, wc=wc):
                                v.wait_ge(sPE.ap, pe_b)
                                if wc.n > 0:
                                    v.wait_ge(wc.ap, wc.n)
                                sDVE.inc(v.tensor_copy(dv, bank[:, 0:256]))
                            dve_e = sDVE.n
                            dve_bdf[b] = dve_e

                            @section("sync")
                            def _(q, b=b, c0=c0, dve_e=dve_e, dv=dv, wc=wc):
                                q.wait_ge(sDVE.ap, dve_e)
                                wc.inc(q.dma_start(
                                    out=wdst[b * 128:(b + 1) * 128,
                                             c0:c0 + 256],
                                    in_=dv))
            if DEBUG:
                @section("sync")
                def _(q):
                    q.wait_ge(sST.ap, sST.n)
                    for wc in sWC:
                        if wc.n > 0:
                            q.wait_ge(wc.ap, wc.n)
                    for c in range(32):
                        sST.inc(q.dma_start(
                            out=dbg["dbg_wconf"][c * 128:(c + 1) * 128, :],
                            in_=wcon_f[c * 128:(c + 1) * 128, :]))
                    sST.inc(q.dma_start(out=dbg["dbg_bexp"], in_=btot))
            barrier()

        def phase_ffn0(p):
            ds_ln = emit_ln_stats(0, T1)
            pe_g4 = {}
            xbc = xln.bitcast(BF16)
            stg = [G4[:, 4096:4608], G4[:, 4608:5120]]
            for ti in range(NT1):
                t0 = ti * T1
                emit_ln_apply(t0, T1, ds_ln)
                @section("vector")
                def _(v):
                    v.wait_ge(sDVE.ap, sDVE.n)
                    for c in range(NCH):
                        sg = stg[c % 2]
                        if c >= 2:
                            v.wait_ge(sDVE.ap, sDVE.n)
                        sDVE.inc(v.tensor_copy(sg, xln[:, c * T1:(c + 1) * T1]))
                        v.wait_ge(sDVE.ap, sDVE.n)
                        sDVE.inc(v.tensor_copy(
                            xbc[:, c * 1024:c * 1024 + 512], sg))
                        v.wait_ge(sDVE.ap, sDVE.n)
                        sDVE.inc(v.tensor_tensor(
                            sg, sg, xbc[:, c * 1024:c * 1024 + 512],
                            ALU.subtract))
                        v.wait_ge(sDVE.ap, sDVE.n)
                        sDVE.inc(v.tensor_copy(
                            xbc[:, c * 1024 + 512:(c + 1) * 1024], sg))
                dve_ln = sDVE.n
                act_base = sACT.n
                pe_fcg = {}
                for fcg in range(16):
                    if fcg == 0:
                        wbuf = G4[:, 0:2048]
                    elif fcg == 1:
                        wbuf = G4[:, 2048:4096]
                    else:
                        wbuf = G5 if fcg % 2 == 0 else G6

                    wsem = sWC[2 + fcg] if fcg < 2 else sWB[fcg % 2]
                    wbc = wbuf[:, 0:2048].bitcast(BF16)

                    @section("sync")
                    def _(q, ti=ti, fcg=fcg, wbuf=wbuf, wsem=wsem, wbc=wbc):
                        if fcg < 2:
                            q.wait_ge(sPE.ap, pe_g4.get((ti - 1, fcg), 0)
                                      if ti > 0 else sPE.n)
                        elif fcg < 4:
                            q.wait_ge(sPE.ap, sPE.n)
                        else:
                            q.wait_ge(sPE.ap, pe_fcg[fcg - 2])
                        wsem.inc(q.dma_start(
                            out=wbc[:, 0:2048].rearrange("p (c n) -> p c n",
                                                         c=NCH),
                            in_=wexp_h.rearrange("(c p) n -> p c n", p=128)
                                [:, :, fcg * 256:(fcg + 1) * 256]))
                        wsem.inc(q.dma_start(
                            out=wbc[:, 2048:4096].rearrange("p (c n) -> p c n",
                                                            c=NCH),
                            in_=wexp_l.rearrange("(c p) n -> p c n", p=128)
                                [:, :, fcg * 256:(fcg + 1) * 256]))
                    ld_w = (wsem, wsem.n)

                    for f2 in range(2):
                        fc = fcg * 2 + f2
                        bank = pb[2 + (fc % 2)]
                        h3buf = G1 if fc < 16 else G2
                        hcol = (fc % 16) * T1

                        wbc = wbuf[:, 0:2048].bitcast(BF16)

                        @section("tensor")
                        def _(t, fc=fc, f2=f2, bank=bank, wbc=wbc,
                              ld_w=ld_w, dve_ln=dve_ln, ab=act_base,
                              hb=None):
                            t.wait_ge(ld_w[0].ap, ld_w[1])
                            if fc == 0:
                                t.wait_ge(sDVE.ap, dve_ln)
                            if fc >= 2:
                                t.wait_ge(sACT.ap, ab + fc - 1)
                            first = True
                            for c in range(NCH):
                                whi = wbc[:, c * 256 + f2 * 128:
                                          c * 256 + f2 * 128 + 128]
                                wlo = wbc[:, 2048 + c * 256 + f2 * 128:
                                          2048 + c * 256 + f2 * 128 + 128]
                                xhi = xbc[:, c * 1024:c * 1024 + 512]
                                xlo = xbc[:, c * 1024 + 512:(c + 1) * 1024]
                                for wop, xop in ((whi, xhi), (whi, xlo),
                                                 (wlo, xhi)):
                                    i = t.matmul(
                                        bank[:, 0:T1], wop, xop,
                                        start=first,
                                        stop=(c == NCH - 1 and wop is wlo))
                                    first = False
                            sPE.inc(i)
                        pe_h = sPE.n
                        pe_fcg[fcg] = pe_h
                        if fcg < 2:
                            pe_g4[(ti, fcg)] = pe_h

                        @section("scalar")
                        def _(a, fc=fc, bank=bank, pe_h=pe_h, h3buf=h3buf,
                              hcol=hcol):
                            a.wait_ge(sPE.ap, pe_h)
                            sACT.inc(a.activation(
                                h3buf[:, hcol:hcol + T1], bank[:, 0:T1],
                                AF.Gelu, bias=btot[:, fc:fc + 1]))
                act_h = sACT.n
                pe_yod = {}
                dve_yo = {}
                for oc in range(NCH):
                    for half in range(2):
                        wbuf = G5 if half == 0 else G6
                        ykey = oc * 2 + half

                        @section("sync")
                        def _(q, oc=oc, half=half, wbuf=wbuf, ykey=ykey):
                            if ykey >= 2:
                                q.wait_ge(sPE.ap, pe_yod[ykey - 2])
                            else:
                                q.wait_ge(sPE.ap, pe_fcg[14 + half])
                            sWB[half].inc(q.dma_start(
                                out=wbuf[:, 0:2048].rearrange(
                                    "p (f n) -> p f n", f=16),
                                in_=wcon_f.rearrange("(f p) n -> p f n", p=128)
                                    [:, half * 16:(half + 1) * 16,
                                     oc * 128:(oc + 1) * 128]))
                        ld_w2 = (sWB[half], sWB[half].n)

                        bank = pb[4 + (oc % 2)]

                        @section("tensor")
                        def _(t, oc=oc, half=half, wbuf=wbuf, bank=bank,
                              ld_w2=ld_w2, act_h=act_h):
                            t.wait_ge(ld_w2[0].ap, ld_w2[1])
                            if oc == 0 and half == 0:
                                t.wait_ge(sACT.ap, act_h)
                            if half == 0 and oc >= 2:
                                t.wait_ge(sDVE.ap, dve_yo[oc - 2])
                            elif half == 0:
                                t.wait_ge(sDVE.ap, sDVE.n)
                            for f in range(16):
                                fc = half * 16 + f
                                h3buf = G1 if fc < 16 else G2
                                i = t.matmul(
                                    bank[:, 0:T1],
                                    wbuf[:, f * 128:(f + 1) * 128],
                                    h3buf[:, (fc % 16) * T1:
                                          (fc % 16) * T1 + T1],
                                    start=(fc == 0), stop=(fc == 31))
                            sPE.inc(i)
                        pe_yod[oc * 2 + half] = sPE.n
                    pe_y = sPE.n

                    @section("vector")
                    def _(v, oc=oc, pe_y=pe_y, t0=t0):
                        bank = pb[4 + (oc % 2)]
                        v.wait_ge(sPE.ap, pe_y)
                        sDVE.inc(v.scalar_tensor_tensor(
                            xres[:, oc * L + t0:oc * L + t0 + T1],
                            bank[:, 0:T1], bcd[:, oc:oc + 1],
                            xres[:, oc * L + t0:oc * L + t0 + T1],
                            ALU.add, ALU.add))
                    dve_yo[oc] = sDVE.n
                if ti + 1 < NT1:
                    ds_ln = emit_ln_stats((ti + 1) * T1, T1, dve_free=dve_ln)
            if DEBUG:
                dve_x2 = sDVE.n

                @section("sync")
                def _(q):
                    q.wait_ge(sDVE.ap, dve_x2)
                    for c in range(NCH):
                        sST.inc(q.dma_start(
                            out=dbg["dbg_x2"][c * 128:(c + 1) * 128, :],
                            in_=xres[:, c * L:(c + 1) * L]))
            barrier()

        def phase_attn1(p):
            TG = 256
            SW = 512
            Sstrip = G1
            attnF = lambda j: xln[:, j * 512 + 256:(j + 1) * 512]
            projTbuf = [G5[:, 0:1024], G5[:, 1024:2048]]
            q_tok = G2[:, 0:2048]
            k_tok = G2[:, 2048:4096]
            qTf = G2[0:64, 4096:6144]
            kTf = G2[:, 6144:7168]
            qscr = G3[0:64, 0:2048]
            OT_odd = G3[64:128, 0:2048]
            Vaug = [G4[:, 0:1040], G4[:, 1040:2080]]
            OT = G4[0:64, 2080:4128]
            qscr_hi = G4[64:128, 2080:4128]
            wsl = [G5[:, 0:1024], G5[:, 1024:2048]]
            maskB = G6[:, 0:2080]

            @section("sync")
            def _(q):
                sOO.inc(q.dma_start(out=maskB, in_=dram["l1_maskbig"]))
            ld_mask = sOO.n

            ds_ln = emit_ln_stats(0, TG)
            pe_g5prev = {}
            for grp in range(L // TG):
                t0 = grp * TG
                emit_ln_apply(t0, TG, ds_ln)
                dve_ln = sDVE.n
                pe_wslice = {}
                for wi, wnm in enumerate(["l1_Wq", "l1_Wk", "l1_Wv"]):
                    for c in range(NCH):
                        wb = wsl[c % 2]
                        wkey = wi * NCH + c

                        @section("sync")
                        def _(q, wi=wi, wnm=wnm, c=c, wb=wb, wkey=wkey,
                              grp=grp):
                            if wkey >= 2:
                                q.wait_ge(sPE.ap, pe_wslice[wkey - 2])
                            elif grp > 0:
                                q.wait_ge(sPE.ap, pe_g5prev[wkey])
                            else:
                                q.wait_ge(sPE.ap, sPE.n)
                            sWB[wkey % 2].inc(q.dma_start(
                                out=wb, in_=dram[wnm][c * 128:(c + 1) * 128, :]))
                        ld_w = (sWB[wkey % 2], sWB[wkey % 2].n)

                        @section("tensor")
                        def _(t, wi=wi, c=c, wb=wb, ld_w=ld_w, dve_ln=dve_ln):
                            t.wait_ge(ld_w[0].ap, ld_w[1])
                            if c == 0:
                                t.wait_ge(sDVE.ap, dve_ln)
                                t.wait_ge(sACT.ap, sACT.n)
                            for u in range(2):
                                for half in range(2):
                                    i = t.matmul(
                                        pb[2 + u * 2 + half][:, 0:512],
                                        xln[:, c * T1 + u * 128:
                                            c * T1 + u * 128 + 128],
                                        wb[:, half * 512:half * 512 + 512],
                                        start=(c == 0), stop=(c == NCH - 1))
                            sPE.inc(i)
                        pe_wslice[wi * NCH + c] = sPE.n
                    pe_w = sPE.n

                    @section("scalar")
                    def _(a, wi=wi, pe_w=pe_w):
                        a.wait_ge(sPE.ap, pe_w)
                        for u in range(2):
                            for half in range(2):
                                src = pb[2 + u * 2 + half][:, 0:512]
                                if wi == 0:
                                    i = a.activation(
                                        q_tok[:, u * 1024 + half * 512:
                                              u * 1024 + half * 512 + 512],
                                        src, AF.Copy, scale=0.125)
                                elif wi == 1:
                                    i = a.activation(
                                        k_tok[:, u * 1024 + half * 512:
                                              u * 1024 + half * 512 + 512],
                                        src, AF.Copy)
                                else:
                                    i = a.activation(
                                        Vaug[u][:, half * 8 * 65:]
                                        .rearrange("p (h r) -> p h r", r=65)
                                        [:, 0:8, 0:64],
                                        src.rearrange("p (h r) -> p h r", r=64),
                                        AF.Copy)
                            sACT.inc(i)
                act_qkv = sACT.n

                @section("vector")
                def _(v, act_qkv=act_qkv):
                    v.wait_ge(sACT.ap, act_qkv)
                    for u in range(2):
                        i = v.memset(Vaug[u].rearrange("p (h r) -> p h r", r=65)
                                     [:, 0:16, 64:65], 1.0)
                    sDVE.inc(i)
                dve_vones = sDVE.n

                for u in range(2):
                    ubase = grp * 2 + u
                    dve_tp = {}
                    for wi in range(2):
                        src = q_tok if wi == 0 else k_tok
                        for i16 in range(16):
                            tidx = wi * 16 + i16
                            odd = (wi == 1 and i16 % 2 == 1)
                            bank = pb[6 + (i16 % 2)]

                            @section("tensor")
                            def _(t, u=u, wi=wi, i16=i16, bank=bank,
                                  act_qkv=act_qkv, src=src, odd=odd,
                                  tidx=tidx):
                                if tidx == 0:
                                    t.wait_ge(sACT.ap, act_qkv)
                                if tidx < 2:
                                    t.wait_ge(sDVE.ap, sDVE.n)
                                else:
                                    t.wait_ge(sDVE.ap, dve_tp[tidx - 2])
                                if odd:
                                    sPE.inc(t.matmul(
                                        bank[64:128, 0:128],
                                        src[:, u * 1024 + i16 * 64:
                                            u * 1024 + i16 * 64 + 64],
                                        ident, start=True, stop=True,
                                        tile_position=(0, 64)))
                                else:
                                    sPE.inc(t.transpose(
                                        bank[0:64, 0:128],
                                        src[:, u * 1024 + i16 * 64:
                                            u * 1024 + i16 * 64 + 64],
                                        ident))
                            pe_t = sPE.n

                            @section("vector")
                            def _(v, wi=wi, i16=i16, bank=bank, pe_t=pe_t,
                                  odd=odd):
                                v.wait_ge(sPE.ap, pe_t)
                                if wi == 0:
                                    sDVE.inc(v.tensor_copy(
                                        qTf[:, i16 * 128:(i16 + 1) * 128],
                                        bank[0:64, 0:128]))
                                elif not odd:
                                    sDVE.inc(v.tensor_copy(
                                        kTf[0:64, (i16 // 2) * 128:
                                            (i16 // 2) * 128 + 128],
                                        bank[0:64, 0:128]))
                                else:
                                    sDVE.inc(v.tensor_copy(
                                        kTf[64:128, (i16 // 2) * 128:
                                            (i16 // 2) * 128 + 128],
                                        bank[64:128, 0:128]))
                            dve_tp[tidx] = sDVE.n
                    dve_tf = sDVE.n

                    @section("vector")
                    def _(v, dve_tf=dve_tf):
                        v.wait_ge(sDVE.ap, dve_tf)
                        if sOO.n > 0:
                            v.wait_ge(sOO.ap, sOO.n)
                        for i16 in range(16):
                            i = v.tensor_copy(
                                qscr.rearrange("p (t h) -> p t h", h=16)
                                [:, 0:128, i16:i16 + 1],
                                qTf[:, i16 * 128:(i16 + 1) * 128]
                                .rearrange("p (t o) -> p t o", o=1))
                        sDVE.inc(i)
                    dve_scr0 = sDVE.n

                    @section("sync")
                    def _(q, dve_scr0=dve_scr0):
                        q.wait_ge(sDVE.ap, dve_scr0)
                        sOO.inc(q.dma_start(out=qscr_hi, in_=qscr))
                    ld_qh = sOO.n

                    dve_scr = dve_scr0

                    pending_tail = []
                    for st8 in range(2048 // SW):
                        PV = min(128, 32 * st8 + 32)
                        dve_tail = 0
                        if pending_tail:
                            pending_tail.pop(0)()
                            dve_tail = sDVE.n
                        dve_sbase = sDVE.n
                        for kp in range(8):
                            bankE = pb[2 + (kp % 2) * 4]
                            bankO = pb[3 + (kp % 2) * 4]

                            @section("tensor")
                            def _(t, st8=st8, kp=kp, bankE=bankE, bankO=bankO,
                                  dve_scr=dve_scr, dve_sbase=dve_sbase,
                                  ld_qh=ld_qh, PV=PV):
                                t.wait_ge(sDVE.ap, dve_scr)
                                if kp == 0:
                                    t.wait_ge(sOO.ap, ld_qh)
                                if kp >= 2:
                                    t.wait_ge(sDVE.ap, dve_sbase + kp - 1)
                                sPE.inc(t.matmul(
                                    bankE[0:PV, 0:SW],
                                    kTf[0:64, kp * 128:kp * 128 + PV],
                                    qscr[:, st8 * SW:(st8 + 1) * SW],
                                    start=True, stop=True))
                                sPE.inc(t.matmul(
                                    bankO[0:PV, 0:SW],
                                    kTf[64:128, kp * 128:kp * 128 + PV],
                                    qscr_hi[:, st8 * SW:(st8 + 1) * SW],
                                    start=True, stop=True,
                                    tile_position=(64, 0)))
                            pe_s = sPE.n

                            @section("vector")
                            def _(v, st8=st8, kp=kp, bankE=bankE, bankO=bankO,
                                  pe_s=pe_s, ld_mask=ld_mask, PV=PV):
                                v.wait_ge(sPE.ap, pe_s)
                                if kp == 0:
                                    v.wait_ge(sOO.ap, ld_mask)
                                for kc, bank in ((2 * kp, bankE),
                                                 (2 * kp + 1, bankO)):
                                    m0 = st8 * SW + 15 - kc
                                    i = v.scalar_tensor_tensor(
                                        Sstrip[0:PV, kc * SW:(kc + 1) * SW],
                                        bank[0:PV, 0:SW], 80.0,
                                        maskB[0:PV, m0:m0 + SW],
                                        ALU.min, ALU.add)
                                sDVE.inc(i)
                        dve_smm = sDVE.n
                        dve_sbase2 = dve_sbase

                        act_eb = sACT.n

                        @section("scalar")
                        def _(a, dve_sbase2=dve_sbase2, PV=PV):
                            for kc in range(16):
                                a.wait_ge(sDVE.ap, dve_sbase2 + kc // 2 + 1)
                                sACT.inc(a.activation(
                                    Sstrip[0:PV, kc * SW:(kc + 1) * SW],
                                    Sstrip[0:PV, kc * SW:(kc + 1) * SW],
                                    AF.Exp))

                        @section("tensor")
                        def _(t, u=u, act_eb=act_eb, dve_vones=dve_vones,
                              PV=PV, dve_tail=dve_tail):
                            t.wait_ge(sDVE.ap, dve_vones)
                            if dve_tail:
                                t.wait_ge(sDVE.ap, dve_tail)
                            else:
                                t.wait_ge(sDVE.ap, sDVE.n)
                            for kc in range(16):
                                t.wait_ge(sACT.ap, act_eb + kc + 1)
                                i = t.matmul(
                                    pb[4][0:65, 0:SW],
                                    Vaug[u][0:PV, kc * 65:(kc + 1) * 65],
                                    Sstrip[0:PV, kc * SW:(kc + 1) * SW],
                                    start=(kc == 0), stop=(kc == 15))
                            sPE.inc(i)
                        pe_o = sPE.n

                        @section("vector")
                        def _(v, pe_o=pe_o):
                            v.wait_ge(sPE.ap, pe_o)
                            sDVE.inc(v.reciprocal(st_nm[0:1, 0:SW],
                                                  pb[4][64:65, 0:SW]))
                        dve_r = sDVE.n

                        def _tail(st8=st8, dve_r=dve_r):
                            @section("tensor")
                            def _(t):
                                t.wait_ge(sDVE.ap, dve_r)
                                sPE.inc(t.matmul(pb[5][0:64, 0:SW],
                                                 onesr[0:1, 0:64],
                                                 st_nm[0:1, 0:SW],
                                                 start=True, stop=True))
                            pe_r = sPE.n

                            @section("vector")
                            def _(v, pe_r=pe_r):
                                v.wait_ge(sPE.ap, pe_r)
                                sDVE.inc(v.tensor_copy(rsb[0:64, 0:SW],
                                                       pb[5][0:64, 0:SW]))
                                v.wait_ge(sDVE.ap, sDVE.n)
                                sDVE.inc(v.scalar_tensor_tensor(
                                    OT[:, st8 * SW:(st8 + 1) * SW],
                                    pb[4][0:64, 0:SW], 1.0, rsb[0:64, 0:SW],
                                    ALU.mult, ALU.mult))
                        pending_tail.append(_tail)
                    while pending_tail:
                        pending_tail.pop(0)()
                    dve_ot = sDVE.n

                    @section("sync")
                    def _(q, dve_ot=dve_ot):
                        q.wait_ge(sDVE.ap, dve_ot)
                        sOO.inc(q.dma_start(out=OT_odd, in_=OT))
                    ld_oo = sOO.n

                    @section("vector")
                    def _(v, u=u, dve_ot=dve_ot, ld_oo=ld_oo):
                        v.wait_ge(sDVE.ap, dve_ot)
                        v.wait_ge(sOO.ap, ld_oo)
                        for j in range(NCH):
                            v.tensor_copy(
                                attnF(j)[0:64, u * 128:u * 128 + 128],
                                OT.rearrange("p (t h) -> p t h", h=16)
                                [:, 0:128, 2 * j:2 * j + 1])
                            i = v.tensor_copy(
                                attnF(j)[64:128, u * 128:u * 128 + 128],
                                OT_odd.rearrange("p (t h) -> p t h", h=16)
                                [:, 0:128, 2 * j + 1:2 * j + 2])
                        sDVE.inc(i)
                dve_attnF = sDVE.n

                pe_pjd = {}
                dve_pj = {}
                for oc in range(NCH):
                    wb = projTbuf[oc % 2]

                    @section("sync")
                    def _(q, oc=oc, wb=wb):
                        if oc >= 2:
                            q.wait_ge(sPE.ap, pe_pjd[oc - 2])
                        else:
                            q.wait_ge(sPE.ap, sPE.n)
                        sWB[oc % 2].inc(q.dma_start(
                            out=wb.rearrange("p (c n) -> p c n", c=NCH),
                            in_=dram["l1_projT"]
                                .rearrange("(c p) n -> p c n", p=128)
                                [:, :, oc * 128:(oc + 1) * 128]))
                    ld_pj = (sWB[oc % 2], sWB[oc % 2].n)

                    bank = pb[6 + (oc % 2)]

                    @section("tensor")
                    def _(t, oc=oc, wb=wb, bank=bank, ld_pj=ld_pj,
                          dve_attnF=dve_attnF):
                        t.wait_ge(ld_pj[0].ap, ld_pj[1])
                        if oc == 0:
                            t.wait_ge(sDVE.ap, dve_attnF)
                        if oc >= 2:
                            t.wait_ge(sDVE.ap, dve_pj[oc - 2])
                        for c in range(NCH):
                            i = t.matmul(bank[:, 0:TG],
                                         wb[:, c * 128:(c + 1) * 128],
                                         attnF(c)[:, 0:256],
                                         start=(c == 0), stop=(c == NCH - 1))
                        sPE.inc(i)
                    pe_pj = sPE.n
                    pe_pjd[oc] = pe_pj
                    if oc >= 6:
                        pe_g5prev[oc - 6] = pe_pj

                    @section("vector")
                    def _(v, oc=oc, bank=bank, pe_pj=pe_pj, t0=t0):
                        v.wait_ge(sPE.ap, pe_pj)
                        sDVE.inc(v.scalar_tensor_tensor(
                            xres[:, oc * L + t0:oc * L + t0 + TG],
                            bank[:, 0:TG], projb[:, oc:oc + 1],
                            xres[:, oc * L + t0:oc * L + t0 + TG],
                            ALU.add, ALU.add))
                    dve_pj[oc] = sDVE.n
                if grp + 1 < L // TG:
                    ds_ln = emit_ln_stats((grp + 1) * TG, TG)
            if DEBUG:
                dve_x3 = sDVE.n

                @section("sync")
                def _(q):
                    q.wait_ge(sDVE.ap, dve_x3)
                    for c in range(NCH):
                        sST.inc(q.dma_start(
                            out=dbg["dbg_x3"][c * 128:(c + 1) * 128, :],
                            in_=xres[:, c * L:(c + 1) * L]))
            barrier()
        def phase_moe():
            h3 = [G1, G2]
            yacc = G3
            wsbc = G4[:, 0:512]
            mx = G4[:, 512:4608]
            ws2buf = G4[0:2, 4608:5120]

            @section("sync")
            def _(q):
                for j in range(8):
                    sLD.inc(q.dma_start(
                        out=a2a_in.ap()[j].rearrange("(c p) n -> p c n", p=128),
                        in_=xres.rearrange("p (c t) -> p c t", c=NCH)
                            [:, :, j * 256:(j + 1) * 256]))
                sOO.inc(q.dma_start(out=ws2buf, in_=dram["l1_ws2"]))
            ld_a2a = sLD.n

            @section("gpsimd")
            def _(g):
                g.wait_ge(sLD.ap, ld_a2a)
                from concourse.aot_env import allow_ici_all_to_all
                with allow_ici_all_to_all():
                    sGP.inc(g.collective_compute(
                        "AllToAll", ALU.bypass,
                        replica_groups=[list(range(NCORE))],
                        ins=[a2a_in.ap().opt()], outs=[a2a_out.ap().opt()]))
            gp_cc = sGP.n

            @section("sync")
            def _(q):
                q.wait_ge(sGP.ap, gp_cc)
                for sdr in range(2):
                    sLD.inc(q.dma_start(
                        out=mx.rearrange("p (c t) -> p c t", c=NCH)
                            [:, :, sdr * 256:(sdr + 1) * 256],
                        in_=a2a_out.ap()[sdr]
                            .rearrange("(c p) n -> p c n", p=128)))
            ld_mx = sLD.n

            for wch in range(1):
                @section("tensor")
                def _(t, ld_ws=sOO.n):
                    t.wait_ge(sOO.ap, ld_ws)
                    sPE.inc(t.matmul(
                        pb[6][:, 0:512], ones2, ws2buf[:, 0:512],
                        start=True, stop=True))
                pe_ws = sPE.n

                @section("vector")
                def _(v, pe_ws=pe_ws):
                    v.wait_ge(sPE.ap, pe_ws)
                    sDVE.inc(v.tensor_copy(wsbc, pb[6][:, 0:512]))
            dve_ws = sDVE.n

            emit_ln(0, T1, src=mx, sstr=T1, ld_wait=ld_mx)
            dve_ln = sDVE.n

            for hf in range(2):
                act_base = sACT.n
                pe_fcg = {}
                for fcg in range(16):
                    wbuf = G5 if fcg % 2 == 0 else G6

                    @section("sync")
                    def _(q, hf=hf, fcg=fcg, wbuf=wbuf):
                        if fcg >= 2:
                            q.wait_ge(sPE.ap, pe_fcg[fcg - 2])
                        else:
                            q.wait_ge(sPE.ap, sPE.n)
                        sWB[fcg % 2].inc(q.dma_start(
                            out=wbuf[:, 0:2048].rearrange(
                                "p (c n) -> p c n", c=NCH),
                            in_=dram["l1_w1"]
                                .rearrange("(c p) n -> p c n", p=128)
                                [:, :, hf * 4096 + fcg * 256:
                                 hf * 4096 + (fcg + 1) * 256]))
                    ld_w = (sWB[fcg % 2], sWB[fcg % 2].n)

                    for f2 in range(2):
                        fc = fcg * 2 + f2
                        bank = pb[2 + (fc % 2)]
                        h3buf = h3[0] if fc < 16 else h3[1]
                        hcol = (fc % 16) * T1

                        @section("tensor")
                        def _(t, fc=fc, f2=f2, bank=bank, wbuf=wbuf,
                              ld_w=ld_w, dve_ln=dve_ln, ab=act_base):
                            t.wait_ge(ld_w[0].ap, ld_w[1])
                            if fc == 0:
                                t.wait_ge(sDVE.ap, dve_ln)
                            if fc >= 2:
                                t.wait_ge(sACT.ap, ab + fc - 1)
                            for c in range(NCH):
                                i = t.matmul(
                                    bank[:, 0:T1],
                                    wbuf[:, c * 256 + f2 * 128:
                                         c * 256 + f2 * 128 + 128],
                                    xln[:, c * T1:(c + 1) * T1],
                                    start=(c == 0), stop=(c == NCH - 1))
                            sPE.inc(i)
                        pe_h = sPE.n
                        pe_fcg[fcg] = pe_h

                        @section("scalar")
                        def _(a, bank=bank, pe_h=pe_h, h3buf=h3buf,
                              hcol=hcol):
                            a.wait_ge(sPE.ap, pe_h)
                            sACT.inc(a.activation(
                                h3buf[:, hcol:hcol + T1], bank[:, 0:T1],
                                AF.Gelu_apprx_tanh))
                act_h = sACT.n
                pe_yod = {}
                for oc in range(NCH):
                    for half2 in range(2):
                        wbuf = G5 if half2 == 0 else G6
                        ykey = oc * 2 + half2

                        @section("sync")
                        def _(q, hf=hf, oc=oc, half2=half2, wbuf=wbuf,
                              ykey=ykey):
                            if ykey >= 2:
                                q.wait_ge(sPE.ap, pe_yod[ykey - 2])
                            else:
                                q.wait_ge(sPE.ap, pe_fcg[14 + half2])
                            sWB[half2].inc(q.dma_start(
                                out=wbuf[:, 0:2048].rearrange(
                                    "p (f n) -> p f n", f=16),
                                in_=dram["l1_w2"]
                                    .rearrange("(f p) n -> p f n", p=128)
                                    [:, hf * 32 + half2 * 16:
                                     hf * 32 + (half2 + 1) * 16,
                                     oc * 128:(oc + 1) * 128]))
                        ld_w2 = (sWB[half2], sWB[half2].n)

                        bank = pb[4 + (oc % 2)]

                        @section("tensor")
                        def _(t, oc=oc, half2=half2, wbuf=wbuf, bank=bank,
                              ld_w2=ld_w2, act_h=act_h):
                            t.wait_ge(ld_w2[0].ap, ld_w2[1])
                            t.wait_ge(sACT.ap, act_h)
                            if half2 == 0:
                                t.wait_ge(sDVE.ap, sDVE.n)
                            for f in range(16):
                                fc = half2 * 16 + f
                                h3buf = h3[0] if fc < 16 else h3[1]
                                i = t.matmul(
                                    bank[:, 0:T1],
                                    wbuf[:, f * 128:(f + 1) * 128],
                                    h3buf[:, (fc % 16) * T1:
                                          (fc % 16) * T1 + T1],
                                    start=(fc == 0), stop=(fc == 31))
                            sPE.inc(i)
                        pe_yod[oc * 2 + half2] = sPE.n
                    pe_y = sPE.n

                    @section("vector")
                    def _(v, hf=hf, oc=oc, pe_y=pe_y):
                        bank = pb[4 + (oc % 2)]
                        v.wait_ge(sPE.ap, pe_y)
                        if hf == 0:
                            sDVE.inc(v.tensor_copy(
                                yacc[:, oc * T1:(oc + 1) * T1],
                                bank[:, 0:T1]))
                        else:
                            sDVE.inc(v.tensor_tensor(
                                yacc[:, oc * T1:(oc + 1) * T1],
                                yacc[:, oc * T1:(oc + 1) * T1],
                                bank[:, 0:T1], ALU.add))
            @section("vector")
            def _(v, dve_ws=dve_ws):
                v.wait_ge(sDVE.ap, dve_ws)
                v.wait_ge(sDVE.ap, sDVE.n)
                for oc in range(NCH):
                    i = v.tensor_tensor(yacc[:, oc * T1:(oc + 1) * T1],
                                        yacc[:, oc * T1:(oc + 1) * T1],
                                        wsbc, ALU.mult)
                sDVE.inc(i)
                v.wait_ge(sDVE.ap, sDVE.n)
                for oc in range(NCH):
                    i = v.tensor_tensor(
                        yacc[:, oc * T1:(oc + 1) * T1],
                        yacc[:, oc * T1:(oc + 1) * T1],
                        mx[:, oc * T1:(oc + 1) * T1], ALU.add)
                sDVE.inc(i)
            dve_moe = sDVE.n

            @section("sync")
            def _(q):
                q.wait_ge(sDVE.ap, dve_moe)
                for c in range(NCH):
                    sST.inc(q.dma_start(
                        out=moe_out[c * 128:(c + 1) * 128, :],
                        in_=yacc[:, c * T1:(c + 1) * T1]))
            barrier()

        def phase_out(p):
            @section("sync")
            def _(q):
                for c in range(NCH):
                    sST.inc(q.dma_start(out=out[p, c * 128:(c + 1) * 128, :],
                                        in_=xres[:, c * L:(c + 1) * L]))
            barrier()

        npass = NPASS
        LIMIT = os.environ.get("KMK_LIMIT", "all")
        for p in range(npass):
            phase_load_x(p)
            phase_kv(p)
            if LIMIT != "kv":
                phase_attn0(p)
            if LIMIT not in ("kv", "attn0"):
                if p == 0:
                    phase_fold()
                phase_ffn0(p)
            if LIMIT not in ("kv", "attn0", "ffn0"):
                phase_attn1(p)
            if LIMIT not in ("kv", "attn0", "ffn0", "attn1"):
                if p == 0:
                    phase_moe()
            phase_out(p)

        @section("gpsimd")
        def _(g):
            g.wait_ge(sST.ap, sST.n)

    return nc



def kernel(**inputs):
    nc = build_program()
    in_maps = [_prep_core_inputs(inputs, c) for c in range(NCORE)]
    res = run_bass_kernel_spmd(nc, in_maps, core_ids=list(range(NCORE)))
    out = np.empty((B, L, D), dtype=np.float32)
    for c in range(NCORE):
        o = res.results[c]["out"]
        out[c] = o[0].T
        out[c + 8] = o[1].T
    for c in range(NCORE):
        mo = res.results[c]["moe_out"]
        out[0][c * 256:(c + 1) * 256] = mo[:, 0:256].T
        out[1][c * 256:(c + 1) * 256] = mo[:, 256:512].T
    return out


